# version 08
# baseline (speedup 1.0000x reference)
import os as _os
_v = _os.environ.get("JAX_PLATFORMS", "")
if _v and "cpu" not in _v.split(","):
    _os.environ["JAX_PLATFORMS"] = _v + ",cpu"

import numpy as np
from contextlib import ExitStack

import concourse.bass as bass
import concourse.mybir as mybir
from concourse.bass_utils import run_bass_kernel_spmd
from concourse.alu_op_type import AluOpType as A

F32 = mybir.dt.float32
I32 = mybir.dt.int32
BF16 = mybir.dt.bfloat16
AF = mybir.ActivationFunctionType

NCORES = 8
B, S, HID = 2, 16384, 2048
H, HKV, D = 16, 8, 128
FFI = 6144
SP = S // NCORES
TP = SP // 16
PT = 512
NPASS = 8

PI_LO = float(np.nextafter(np.float32(np.pi), np.float32(0.0)))
TWO_PI = 2.0 * np.pi
C1 = float(np.float32(TWO_PI))
C2 = float(np.float32(np.float64(TWO_PI) - np.float64(np.float32(TWO_PI))))
HALF_PI = float(np.float32(np.pi / 2))
HPI_ADJ = float(np.float32(np.float64(np.float32(np.pi / 2))
                           - np.float64(np.pi) / 2))


class Walk:

    ENGINES = ("sync", "vector", "scalar", "tensor", "gpsimd")

    def __init__(self, nc, sems, counts=None):
        self.nc = nc
        self.prog = {e: [] for e in self.ENGINES}
        self.sems = sems
        self.count = dict(counts) if counts else {k: 0 for k in sems}

    def emit(self, eng, fn):
        self.prog[eng].append(fn)

    def inc(self, eng, mk, sem, n):
        s = self.sems[sem]
        self.emit(eng, lambda e, mk=mk, s=s, n=n: mk(e).then_inc(s, n))
        self.count[sem] += n

    def bare(self, eng, mk):
        self.emit(eng, lambda e, mk=mk: mk(e))

    def inc_last(self, eng, sem, n):
        s = self.sems[sem]
        old = self.prog[eng].pop()
        self.emit(eng, lambda e, old=old, s=s, n=n: old(e).then_inc(s, n))
        self.count[sem] += n

    def wait(self, eng, sem, val=None):
        s = self.sems[sem]
        v = self.count[sem] if val is None else val
        if v > 0:
            self.emit(eng, lambda e, s=s, v=v: e.wait_ge(s, v))

    def run_block(self, block):
        for name in self.ENGINES:
            closures = self.prog[name]
            if not closures:
                continue

            def body(eng, closures=closures):
                for fn in closures:
                    fn(eng)

            getattr(block, name)(body)


def _rsqrt_newton(w, src, dst, scr, itile, rows, cols, pad=None):
    if pad is None:
        pad = lambda: None
    c = (slice(0, rows), slice(0, cols))
    w.bare("vector", lambda e: e.tensor_scalar(
        out=itile[c], in0=src[c].bitcast(I32), scalar1=1, scalar2=None,
        op0=A.logical_shift_right))
    pad()
    w.bare("vector", lambda e: e.tensor_scalar(
        out=itile[c], in0=itile[c], scalar1=-1, scalar2=float(0x5f3759df),
        op0=A.mult, op1=A.add))
    pad()
    w.bare("vector", lambda e: e.tensor_scalar(
        out=dst[c], in0=itile[c].bitcast(F32), scalar1=1.0, scalar2=None,
        op0=A.mult))
    pad()
    for _ in range(4):
        w.bare("vector", lambda e: e.tensor_tensor(
            out=scr[c], in0=src[c], in1=dst[c], op=A.mult))
        pad()
        w.bare("vector", lambda e: e.tensor_tensor(
            out=scr[c], in0=scr[c], in1=dst[c], op=A.mult))
        pad()
        w.bare("vector", lambda e: e.tensor_scalar(
            out=scr[c], in0=scr[c], scalar1=-0.5, scalar2=1.5,
            op0=A.mult, op1=A.add))
        pad()
        w.bare("vector", lambda e: e.tensor_tensor(
            out=dst[c], in0=dst[c], in1=scr[c], op=A.mult))
        pad()


def _quant_block(w, wf, it1, it2, it3, ft1, ft2, ft3, ft4, ft5, mki, out_bf,
                 ncols):
    sl = (slice(None), slice(0, ncols))
    bits = wf[sl].bitcast(I32)
    w.bare("vector", lambda e: e.tensor_scalar(
        out=it1[sl], in0=bits, scalar1=0x7FFFFF, scalar2=None, op0=A.bitwise_and))
    w.bare("vector", lambda e: e.tensor_scalar(
        out=it1[sl], in0=it1[sl], scalar1=float(0x7FFFF), scalar2=None, op0=A.add))
    w.bare("vector", lambda e: e.tensor_scalar(
        out=it2[sl], in0=bits, scalar1=20, scalar2=1,
        op0=A.logical_shift_right, op1=A.bitwise_and))
    w.bare("vector", lambda e: e.tensor_tensor(
        out=it1[sl], in0=it1[sl], in1=it2[sl], op=A.add))
    w.bare("vector", lambda e: e.tensor_scalar(
        out=it1[sl], in0=it1[sl], scalar1=20, scalar2=20,
        op0=A.logical_shift_right, op1=A.logical_shift_left))
    w.bare("vector", lambda e: e.tensor_scalar(
        out=it3[sl], in0=bits, scalar1=23, scalar2=23,
        op0=A.logical_shift_right, op1=A.logical_shift_left))
    w.bare("vector", lambda e: e.tensor_tensor(
        out=it3[sl], in0=it3[sl], in1=it1[sl], op=A.add))
    w.bare("vector", lambda e: e.tensor_copy(out=mki[sl], in_=ft1[sl]))
    w.bare("vector", lambda e: e.tensor_copy(out=ft2[sl], in_=mki[sl]))
    w.bare("vector", lambda e: e.tensor_tensor(
        out=ft3[sl], in0=ft1[sl], in1=ft2[sl], op=A.subtract))
    w.bare("vector", lambda e: e.tensor_scalar(
        out=ft4[sl], in0=ft3[sl], scalar1=0.5, scalar2=None, op0=A.is_gt))
    w.bare("vector", lambda e: e.tensor_tensor(
        out=ft2[sl], in0=ft2[sl], in1=ft4[sl], op=A.add))
    w.bare("vector", lambda e: e.tensor_scalar(
        out=ft4[sl], in0=ft3[sl], scalar1=0.5, scalar2=None, op0=A.is_equal))
    w.bare("vector", lambda e: e.tensor_scalar(
        out=it2[sl], in0=mki[sl], scalar1=1, scalar2=None, op0=A.bitwise_and))
    w.bare("vector", lambda e: e.tensor_copy(out=ft3[sl], in_=it2[sl]))
    w.bare("vector", lambda e: e.tensor_tensor(
        out=ft4[sl], in0=ft4[sl], in1=ft3[sl], op=A.mult))
    w.bare("vector", lambda e: e.tensor_tensor(
        out=ft2[sl], in0=ft2[sl], in1=ft4[sl], op=A.add))
    w.bare("vector", lambda e: e.tensor_scalar(
        out=ft2[sl], in0=ft2[sl], scalar1=float(2.0 ** -9), scalar2=None,
        op0=A.mult))
    w.bare("vector", lambda e: e.tensor_tensor(
        out=ft2[sl], in0=ft2[sl], in1=ft5[sl], op=A.mult))
    w.bare("vector", lambda e: e.tensor_scalar(
        out=mki[sl], in0=bits, scalar1=23, scalar2=255,
        op0=A.logical_shift_right, op1=A.bitwise_and))
    w.bare("vector", lambda e: e.tensor_scalar(
        out=mki[sl], in0=mki[sl], scalar1=120, scalar2=None, op0=A.is_le))
    w.bare("vector", lambda e: e.select(
        out=out_bf[sl], mask=mki[sl], on_true=ft2[sl],
        on_false=it3[sl].bitcast(F32)))


def build_program():
    nc = bass.Bass("TRN2", target_bir_lowering=False,
                   detect_race_conditions=False)

    d_xa = nc.dram_tensor("xa", [B, SP, HID], F32, kind="ExternalInput").ap()
    d_xf = nc.dram_tensor("xf", [B, SP, HID], F32, kind="ExternalInput").ap()
    d_wqkvT = nc.dram_tensor("wqkvT", [HID, 4096], F32, kind="ExternalInput").ap()
    d_woT = nc.dram_tensor("woT", [HID, HID], F32, kind="ExternalInput").ap()
    d_gw = nc.dram_tensor("gw", [HID, FFI], F32, kind="ExternalInput").ap()
    d_uw = nc.dram_tensor("uw", [HID, FFI], F32, kind="ExternalInput").ap()
    d_dw = nc.dram_tensor("dw", [FFI, HID], F32, kind="ExternalInput").ap()
    d_cosf = nc.dram_tensor("cosf", [128, SP], F32, kind="ExternalInput").ap()
    d_ssinf = nc.dram_tensor("ssinf", [128, SP], F32, kind="ExternalInput").ap()
    d_rT = nc.dram_tensor("rT", [128, 64], F32, kind="ExternalInput").ap()
    d_maskA = nc.dram_tensor("maskA", [128, 256], F32, kind="ExternalInput").ap()
    d_iden = nc.dram_tensor("iden", [128, 128], F32, kind="ExternalInput").ap()
    d_sel = nc.dram_tensor("sel", [128, 24 * 32], F32, kind="ExternalInput").ap()
    d_bq = nc.dram_tensor("bq", [128, 16], F32, kind="ExternalInput").ap()
    d_bk = nc.dram_tensor("bk", [128, 8], F32, kind="ExternalInput").ap()
    d_bv = nc.dram_tensor("bv", [128, 8], F32, kind="ExternalInput").ap()
    d_bo = nc.dram_tensor("bo", [128, 16], F32, kind="ExternalInput").ap()
    d_gs = nc.dram_tensor("gs", [128, 48], F32, kind="ExternalInput").ap()
    d_us = nc.dram_tensor("us", [128, 48], F32, kind="ExternalInput").ap()
    d_ds = nc.dram_tensor("ds", [128, 16], F32, kind="ExternalInput").ap()

    d_out = nc.dram_tensor("out", [B, SP, HID], F32, kind="ExternalOutput").ap()
    s_wqkv = nc.dram_tensor("s_wqkv", [HID, 4096], BF16, kind="Internal").ap()
    s_woT = nc.dram_tensor("s_woT", [HID, HID], BF16, kind="Internal").ap()
    s_gq = nc.dram_tensor("s_gq", [HID, FFI], BF16, kind="Internal").ap()
    s_uq = nc.dram_tensor("s_uq", [HID, FFI], BF16, kind="Internal").ap()
    s_dq = nc.dram_tensor("s_dq", [FFI, HID], BF16, kind="Internal").ap()
    s_sat = nc.dram_tensor("s_sat", [NPASS, 16, 128, PT], F32,
                           kind="Internal").ap()
    s_rsq = nc.dram_tensor("s_rsq", [32, PT], F32, kind="Internal").ap()

    with ExitStack() as ctx:
        sbt = lambda n, s, d: ctx.enter_context(nc.sbuf_tensor(n, s, d))
        cosf = sbt("cosf_t", [128, SP], F32)
        ssinf = sbt("ssinf_t", [128, SP], F32)
        rT_b = sbt("rT_b", [128, 64], BF16)
        maskA_b = sbt("maskA_b", [128, 256], BF16)
        iden_b = sbt("iden_b", [128, 128], BF16)
        iden_f = sbt("iden_f", [128, 128], F32)
        sel_b = sbt("sel_b", [128, 24 * 32], BF16)
        bq_sb = sbt("bq_sb", [128, 16], F32)
        bk_sb = sbt("bk_sb", [128, 8], F32)
        bv_sb = sbt("bv_sb", [128, 8], F32)
        bo_sb = sbt("bo_sb", [128, 16], F32)
        gs_sb = sbt("gs_sb", [128, 48], F32)
        us_sb = sbt("us_sb", [128, 48], F32)
        ds_sb = sbt("ds_sb", [128, 16], F32)
        cst_hpi = sbt("cst_hpi", [128, 1], F32)
        cst_adj = sbt("cst_adj", [128, 1], F32)
        fv = sbt("fence_v", [1, 8], F32)
        nc.const_aps.register(F32, HALF_PI, cst_hpi[:, 0:1])
        nc.const_aps.register(F32, HPI_ADJ, cst_adj[:, 0:1])

        sems = {k: ctx.enter_context(nc.semaphore(k)) for k in
                ("s_ld", "s_w", "s_mm", "s_v", "s_s", "s_st", "s_g", "s_c")}

        def pad2():
            for _ in range(2):
                w.bare("vector", lambda e: e.tensor_scalar(
                    out=fv[0:1, 0:4], in0=fv[0:1, 4:8], scalar1=0.0,
                    scalar2=None, op0=A.mult))

        with ExitStack() as ctxp:
            psb = lambda n, s, d: ctxp.enter_context(nc.sbuf_tensor(n, s, d))
            QW = 3072
            wf = psb("p_wf", [128, QW], F32)
            wb = psb("p_wb", [128, QW], BF16)
            it1 = psb("p_it1", [128, QW], I32)
            it2 = psb("p_it2", [128, QW], I32)
            it3 = psb("p_it3", [128, QW], I32)
            ft1 = psb("p_ft1", [128, QW], F32)
            ft2 = psb("p_ft2", [128, QW], F32)
            ft3 = psb("p_ft3", [128, QW], F32)
            ft4 = psb("p_ft4", [128, QW], F32)
            ft5 = psb("p_ft5", [128, QW], F32)
            mki = psb("p_mki", [128, QW], I32)
            cb = psb("p_cb", [128, 4096], BF16)

            w = Walk(nc, sems)
            w.inc("vector", lambda e: e.memset(cst_hpi[:], HALF_PI), "s_c", 1)
            w.inc("vector", lambda e: e.memset(cst_adj[:], HPI_ADJ), "s_c", 1)
            w.wait("scalar", "s_c")
            w.bare("vector", lambda e: e.memset(fv[:], 0.0))
            for dst, src in ((cosf, d_cosf), (ssinf, d_ssinf),
                             (bq_sb, d_bq), (bk_sb, d_bk), (bv_sb, d_bv),
                             (bo_sb, d_bo), (gs_sb, d_gs), (us_sb, d_us),
                             (ds_sb, d_ds), (iden_f, d_iden)):
                w.inc("sync", lambda e, dst=dst, src=src: e.dma_start(
                    out=dst[:], in_=src[:]), "s_ld", 16)
            for dst, src in ((rT_b, d_rT), (maskA_b, d_maskA),
                             (iden_b, d_iden), (sel_b, d_sel)):
                w.inc("gpsimd", lambda e, dst=dst, src=src: e.dma_start(
                    out=dst[:], in_=src[:]), "s_g", 16)

            def conv_stream(src_ap, dst_ap, nblk, width):
                for i in range(nblk):
                    w.inc("gpsimd", lambda e, i=i, src=src_ap, wd=width:
                          e.dma_start(out=cb[:, 0:wd],
                                      in_=src[i * 128:(i + 1) * 128, :]),
                          "s_g", 16)
                    w.wait("sync", "s_g")
                    w.inc("sync", lambda e, i=i, dst=dst_ap, wd=width:
                          e.dma_start(out=dst[i * 128:(i + 1) * 128, :],
                                      in_=cb[:, 0:wd]),
                          "s_st", 16)
                    w.wait("gpsimd", "s_st")

            conv_stream(d_wqkvT, s_wqkv, 16, 4096)
            conv_stream(d_woT, s_woT, 16, 2048)

            def quant_stream(src_ap, dst_ap, nblk, tot_width):
                nchunk = (tot_width + 3071) // 3072
                for i0 in range(nblk * nchunk):
                    i, cw = i0 // nchunk, i0 % nchunk
                    width = min(3072, tot_width - cw * 3072)
                    c0 = cw * 3072
                    w.inc("sync", lambda e, i=i, c0=c0, src=src_ap, wd=width:
                          e.dma_start(out=wf[:, 0:wd],
                                      in_=src[i * 128:(i + 1) * 128,
                                              c0:c0 + wd]),
                          "s_ld", 16)
                    w.wait("scalar", "s_ld")
                    w.bare("scalar", lambda e, wd=width: e.activation(
                        out=ft1[:, 0:wd], in_=wf[:, 0:wd], func=AF.Abs,
                        scale=512.0))
                    w.bare("scalar", lambda e, wd=width: e.activation(
                        out=ft5[:, 0:wd], in_=wf[:, 0:wd], func=AF.Sign))
                    w.inc_last("scalar", "s_s", 1)
                    w.wait("vector", "s_s")
                    w.wait("vector", "s_ld")
                    _quant_block(w, wf, it1, it2, it3, ft1, ft2, ft3, ft4,
                                 ft5, mki, wb, width)
                    w.inc_last("vector", "s_v", 1)
                    w.wait("sync", "s_v")
                    w.inc("sync", lambda e, i=i, c0=c0, dst=dst_ap, wd=width:
                          e.dma_start(out=dst[i * 128:(i + 1) * 128,
                                              c0:c0 + wd],
                                      in_=wb[:, 0:wd]),
                          "s_st", 16)
                    w.wait("sync", "s_st")
                    w.wait("scalar", "s_st", w.count["s_st"])
                    w.wait("vector", "s_st", w.count["s_st"])

            quant_stream(d_gw, s_gq, 16, FFI)
            quant_stream(d_uw, s_uq, 16, FFI)
            quant_stream(d_dw, s_dq, 48, HID)
            w.wait("sync", "s_st")
            w.wait("sync", "s_ld")
            w.wait("gpsimd", "s_g")

            with nc.Block() as blk:
                w.run_block(blk)
            counts = dict(w.count)

        with ExitStack() as ctxa:
            asb = lambda n, s, d: ctxa.enter_context(nc.sbuf_tensor(n, s, d))
            xg = asb("a_xg", [128, HID], F32)
            xsq = asb("a_xsq", [128, HID], F32)
            sx = asb("a_sx", [128, 1], F32)
            sxx = asb("a_sxx", [128, 1], F32)
            lt1 = asb("a_lt1", [128, 1], F32)
            lt2 = asb("a_lt2", [128, 1], F32)
            lrs = asb("a_lrs", [128, 1], F32)
            lit = asb("a_lit", [128, 1], I32)
            xn_tm = asb("a_xntm", [128, HID], BF16)
            xn_fm = asb("a_xnfm", [128, 16 * PT], BF16)
            wbuf = [asb(f"a_wbuf{i}", [128, 640], BF16) for i in range(2)]
            qa = asb("a_qa", [128, PT], F32)
            qb = asb("a_qb", [128, PT], F32)
            t1 = asb("a_t1", [128, PT], F32)
            t2 = asb("a_t2", [128, PT], F32)
            sqb = asb("a_sqb", [128, PT], BF16)
            qrot = asb("a_qrot", [128, 24 * PT], BF16)
            v_il = asb("a_vil", [128, 8 * PT], BF16)
            ssqf = asb("a_ssqf", [32, PT], F32)
            rsq = asb("a_rsq", [32, PT], F32)
            nsc = asb("a_nsc", [32, PT], F32)
            nit = asb("a_nit", [32, PT], I32)
            qhat = asb("a_qhat", [128, PT], BF16)
            ppail = asb("a_pp", [128, PT], F32)
            tg1 = asb("a_tg1", [128, PT], F32)
            tg2 = asb("a_tg2", [128, PT], F32)
            tga = asb("a_tga", [64, PT], F32)
            tga2 = asb("a_tga2", [64, PT], F32)
            qp_il = asb("a_qpil", [128, 16 * PT], BF16)
            kp_il = asb("a_kpil", [128, 8 * PT], BF16)
            abuf = asb("a_abuf", [128, 256], BF16)
            bcb = asb("a_bcb", [128, PT], F32)
            vT = asb("a_vT", [128, 128], BF16)
            attn_srt = asb("a_srt", [128, 16 * PT], BF16)
            sbf_s = asb("a_sbf", [128, PT], F32)

            ps_big = ctxa.enter_context(
                nc.psum_tensor("a_psbig", [128, 5 * PT], F32))
            ps_ssq = ctxa.enter_context(
                nc.psum_tensor("a_psssq", [32, PT], F32))
            ps_aux = ctxa.enter_context(
                nc.psum_tensor("a_psaux", [128, PT], F32))
            ps_trb = ctxa.enter_context(
                nc.psum_tensor("a_pstrb", [128, 128], F32))

            w = Walk(nc, sems, counts)

            QKV_GROUPS = [(0, 5), (5, 5), (10, 5), (15, 5), (20, 5), (25, 5),
                          (30, 2)]
            WO_GROUPS = [(0, 5), (5, 5), (10, 5), (15, 1)]

            def ln_block(x_tile, xn_tile, sq_scratch):
                w.bare("scalar", lambda e: e.activation(
                    out=sq_scratch[:], in_=x_tile[:], func=AF.Square,
                    accum_out=sxx[:]))
                w.bare("scalar", lambda e: e.activation(
                    out=sq_scratch[:], in_=x_tile[:], func=AF.Identity,
                    accum_out=sx[:]))
                w.inc_last("scalar", "s_s", 1)
                w.wait("vector", "s_s")
                w.bare("vector", lambda e: e.tensor_scalar(
                    out=lt1[:], in0=sx[:], scalar1=1.0 / 2048.0, scalar2=None,
                    op0=A.mult))
                pad2()
                w.bare("vector", lambda e: e.tensor_tensor(
                    out=lt2[:], in0=lt1[:], in1=lt1[:], op=A.mult))
                pad2()
                w.bare("vector", lambda e: e.scalar_tensor_tensor(
                    out=lt2[:], in0=sxx[:], scalar=1.0 / 2048.0, in1=lt2[:],
                    op0=A.mult, op1=A.subtract))
                pad2()
                w.bare("vector", lambda e: e.tensor_scalar(
                    out=lt2[:], in0=lt2[:], scalar1=1e-5, scalar2=None,
                    op0=A.add))
                pad2()
                _rsqrt_newton(w, lt2, lrs, lt1, lit, 128, 1, pad=pad2)
                w.bare("vector", lambda e: e.tensor_scalar(
                    out=lt1[:], in0=sx[:], scalar1=-1.0 / 2048.0, scalar2=None,
                    op0=A.mult))
                pad2()
                w.bare("vector", lambda e: e.tensor_tensor(
                    out=lt1[:], in0=lt1[:], in1=lrs[:], op=A.mult))
                pad2()
                w.bare("vector", lambda e: e.tensor_scalar(
                    out=xn_tile[:], in0=x_tile[:], scalar1=lrs[:, 0:1],
                    scalar2=lt1[:, 0:1], op0=A.mult, op1=A.add))

            for p in range(NPASS):
                bidx = p // 4
                s0 = (p % 4) * PT

                for g in range(4):
                    w.wait("sync", "s_v")
                    w.wait("sync", "s_s", w.count["s_s"])
                    w.inc("sync", lambda e, b=bidx, r=s0 + g * 128:
                          e.dma_start(out=xg[:], in_=d_xa[b, r:r + 128, :]),
                          "s_ld", 16)
                    w.wait("vector", "s_ld")
                    w.wait("scalar", "s_ld", w.count["s_ld"])
                    ln_block(xg, xn_tm, xsq)
                    w.inc_last("vector", "s_v", 1)
                    w.wait("tensor", "s_v")
                    for ck in range(16):
                        w.inc("tensor", lambda e, ck=ck: e.matmul(
                            ps_trb[:], xn_tm[:, ck * 128:(ck + 1) * 128],
                            iden_b[:], start=True, stop=True), "s_mm", 1)
                        w.wait("vector", "s_mm")
                        w.inc("vector", lambda e, ck=ck, g=g: e.tensor_copy(
                            out=xn_fm[:, ck * PT + g * 128:ck * PT + g * 128 + 128],
                            in_=ps_trb[:]), "s_v", 1)
                        w.wait("tensor", "s_v")

                nqk = 0
                for (oc0, gn) in QKV_GROUPS:
                    for ck in range(16):
                        buf = wbuf[ck % 2]
                        w.wait("sync", "s_mm")
                        w.inc("sync", lambda e, ck=ck, oc0=oc0, gn=gn, buf=buf:
                              e.dma_start(
                                  out=buf[:, 0:gn * 128],
                                  in_=s_wqkv[ck * 128:(ck + 1) * 128,
                                             oc0 * 128:(oc0 + gn) * 128]),
                              "s_w", 16)
                        w.wait("tensor", "s_w")
                        for oi in range(gn):
                            w.inc("tensor", lambda e, ck=ck, oi=oi, buf=buf:
                                  e.matmul(
                                      ps_big[:, oi * PT:(oi + 1) * PT],
                                      buf[:, oi * 128:(oi + 1) * 128],
                                      xn_fm[:, ck * PT:(ck + 1) * PT],
                                      start=(ck == 0), stop=(ck == 15)),
                                  "s_mm", 1)
                    w.wait("scalar", "s_mm")
                    w.wait("vector", "s_mm", w.count["s_mm"])
                    for oi in range(gn):
                        oc = oc0 + oi
                        pslice = ps_big[:, oi * PT:(oi + 1) * PT]
                        if oc < 24:
                            bias_ap = (bq_sb[:, oc:oc + 1] if oc < 16
                                       else bk_sb[:, oc - 16:oc - 15])
                            w.wait("scalar", "s_v", w.count["s_v"])
                            w.bare("scalar", lambda e, ps=pslice, ba=bias_ap:
                                   e.activation(out=qa[:], in_=ps,
                                                func=AF.Identity, bias=ba))
                            w.inc_last("scalar", "s_s", 1)
                            w.wait("vector", "s_s")
                            w.bare("vector", lambda e: e.tensor_scalar(
                                out=qb[0:64, :], in0=qa[64:128, :],
                                scalar1=1.0, scalar2=None, op0=A.mult))
                            w.bare("vector", lambda e: e.tensor_scalar(
                                out=qb[64:128, :], in0=qa[0:64, :],
                                scalar1=1.0, scalar2=None, op0=A.mult))
                            cs = cosf[:, s0:s0 + PT]
                            ss = ssinf[:, s0:s0 + PT]
                            w.bare("vector", lambda e, cs=cs: e.tensor_tensor(
                                out=t1[:], in0=qa[:], in1=cs, op=A.mult))
                            w.bare("vector", lambda e, ss=ss: e.tensor_tensor(
                                out=t2[:], in0=qb[:], in1=ss, op=A.mult))
                            w.bare("vector", lambda e: e.tensor_tensor(
                                out=t1[:], in0=t1[:], in1=t2[:], op=A.add))
                            w.bare("vector", lambda e, oc=oc: e.tensor_copy(
                                out=qrot[:, oc * PT:(oc + 1) * PT], in_=t1[:]))
                            w.bare("vector", lambda e: e.tensor_tensor(
                                out=sqb[:], in0=t1[:], in1=t1[:], op=A.mult))
                            w.inc_last("vector", "s_v", 1)
                            w.wait("tensor", "s_v")
                            w.inc("tensor", lambda e, k=nqk: e.matmul(
                                ps_ssq[:], sel_b[:, k * 32:(k + 1) * 32],
                                sqb[:], start=(k == 0), stop=(k == 23)),
                                "s_mm", 1)
                            nqk += 1
                        else:
                            j = oc - 24
                            w.inc("scalar", lambda e, ps=pslice, j=j:
                                  e.activation(
                                      out=v_il[:].rearrange(
                                          "p (t j) -> p t j", j=8)[:, :, j],
                                      in_=ps, func=AF.Identity,
                                      bias=bv_sb[:, j:j + 1]), "s_s", 1)
                    w.inc("vector", lambda e: e.tensor_scalar(
                        out=fv[0:1, 0:4], in0=fv[0:1, 4:8], scalar1=0.0,
                        scalar2=None, op0=A.mult), "s_v", 1)
                    w.wait("tensor", "s_v")
                    w.wait("tensor", "s_s", w.count["s_s"])

                w.wait("vector", "s_mm", w.count["s_mm"])
                w.bare("vector", lambda e: e.tensor_scalar(
                    out=ssqf[:], in0=ps_ssq[:], scalar1=1e-12, scalar2=None,
                    op0=A.max))
                _rsqrt_newton(w, ssqf, rsq, nsc, nit, 32, PT)
                w.inc_last("vector", "s_v", 1)
                w.wait("sync", "s_v")
                w.inc("sync", lambda e: e.dma_start(out=s_rsq[:], in_=rsq[:]),
                      "s_st", 16)
                w.wait("gpsimd", "s_st", w.count["s_st"])

                for m in range(24):
                    pair, half = m // 2, m % 2
                    w.wait("gpsimd", "s_v", w.count["s_v"])
                    w.inc("gpsimd", lambda e, m=m: e.dma_start(
                        out=bcb[:],
                        in_=s_rsq[m:m + 1, :].broadcast_to((128, PT))),
                        "s_g", 16)
                    w.wait("vector", "s_g")
                    w.inc("vector", lambda e, m=m: e.tensor_tensor(
                        out=qhat[:], in0=qrot[:, m * PT:(m + 1) * PT],
                        in1=bcb[:], op=A.mult), "s_v", 1)
                    w.wait("tensor", "s_v")
                    w.inc("tensor", lambda e, half=half: e.matmul(
                        ps_big[half * 64:(half + 1) * 64,
                               half * PT:(half + 1) * PT],
                        rT_b[:], qhat[:], start=True, stop=True), "s_mm", 1)
                    if half == 1:
                        w.wait("vector", "s_mm")
                        w.bare("vector", lambda e: e.tensor_scalar(
                            out=ppail[0:64, :], in0=ps_big[0:64, 0:PT],
                            scalar1=1.0, scalar2=None, op0=A.mult))
                        w.inc("vector", lambda e: e.tensor_scalar(
                            out=ppail[64:128, :], in0=ps_big[64:128, PT:2 * PT],
                            scalar1=1.0, scalar2=None, op0=A.mult), "s_v", 1)
                        w.wait("tensor", "s_v")
                        w.bare("vector", lambda e: e.tensor_scalar(
                            out=tg1[:], in0=ppail[:], scalar1=PI_LO,
                            scalar2=None, op0=A.is_gt))
                        w.bare("vector", lambda e: e.tensor_scalar(
                            out=tg2[:], in0=ppail[:], scalar1=-PI_LO,
                            scalar2=None, op0=A.is_lt))
                        w.bare("vector", lambda e: e.tensor_tensor(
                            out=tg1[:], in0=tg1[:], in1=tg2[:], op=A.subtract))
                        w.bare("vector", lambda e: e.scalar_tensor_tensor(
                            out=tg2[:], in0=tg1[:], scalar=-C1, in1=ppail[:],
                            op0=A.mult, op1=A.add))
                        w.bare("vector", lambda e: e.scalar_tensor_tensor(
                            out=tg2[:], in0=tg1[:], scalar=-C2, in1=tg2[:],
                            op0=A.mult, op1=A.add))
                        w.bare("vector", lambda e: e.tensor_scalar(
                            out=tg2[:], in0=tg2[:], scalar1=-PI_LO,
                            scalar2=PI_LO, op0=A.max, op1=A.min))
                        w.inc_last("vector", "s_v", 1)
                        w.wait("scalar", "s_v")
                        for hh in range(2):
                            hm = pair * 2 + hh
                            if hm < 16:
                                dstt = qp_il[:].rearrange(
                                    "p (t h) -> p t h", h=16)
                                sin_dst = dstt[0:64, :, hm]
                                cos_dst = dstt[64:128, :, hm]
                            else:
                                dstt = kp_il[:].rearrange(
                                    "p (t j) -> p t j", j=8)
                                sin_dst = dstt[0:64, :, hm - 16]
                                cos_dst = dstt[64:128, :, hm - 16]
                            rrs = tg2[hh * 64:(hh + 1) * 64, :]
                            w.bare("scalar", lambda e, rrs=rrs, o=sin_dst:
                                   e.activation(out=o, in_=rrs, func=AF.Sin))
                            w.bare("scalar", lambda e, rrs=rrs: e.activation(
                                out=tga[:], in_=rrs, func=AF.Abs))
                            w.bare("scalar", lambda e: e.activation(
                                out=tga2[:], in_=tga[:], func=AF.Identity,
                                bias=HPI_ADJ))
                            w.bare("scalar", lambda e, o=cos_dst: e.activation(
                                out=o, in_=tga2[:], func=AF.Sin, scale=-1.0,
                                bias=HALF_PI))
                        w.inc_last("scalar", "s_s", 1)
                        w.wait("vector", "s_s")
                        w.wait("tensor", "s_s", w.count["s_s"])

                for g in range(32):
                    w.inc("tensor", lambda e, g=g: e.matmul(
                        ps_big[:, 0:256], kp_il[:, g * 128:(g + 1) * 128],
                        qp_il[:, g * 256:(g + 1) * 256], start=True,
                        stop=True), "s_mm", 1)
                    w.wait("vector", "s_mm")
                    w.inc("vector", lambda e: e.tensor_tensor(
                        out=abuf[:], in0=ps_big[:, 0:256], in1=maskA_b[:],
                        op=A.mult), "s_v", 1)
                    w.inc("tensor", lambda e, g=g: e.matmul(
                        ps_trb[:], v_il[:, g * 128:(g + 1) * 128],
                        iden_b[:], start=True, stop=True), "s_mm", 1)
                    w.wait("vector", "s_mm")
                    w.inc("vector", lambda e: e.tensor_copy(
                        out=vT[:], in_=ps_trb[:]), "s_v", 1)
                    w.wait("tensor", "s_v")
                    w.inc("tensor", lambda e: e.matmul(
                        ps_aux[:, 0:256], vT[:], abuf[:], start=True,
                        stop=True), "s_mm", 1)
                    w.wait("scalar", "s_mm")
                    srt4 = attn_srt[:].rearrange(
                        "p (ti g2 h) -> p ti g2 h", ti=16, g2=32)
                    w.inc("scalar", lambda e, g=g, d4=srt4: e.activation(
                        out=d4[:, :, g, :], in_=ps_aux[:, 0:256],
                        func=AF.Identity), "s_s", 1)
                    w.wait("tensor", "s_s", w.count["s_s"])

                for (oc0, gn) in WO_GROUPS:
                    for ck in range(16):
                        buf = wbuf[ck % 2]
                        w.wait("sync", "s_mm")
                        w.inc("sync", lambda e, ck=ck, oc0=oc0, gn=gn, buf=buf:
                              e.dma_start(
                                  out=buf[:, 0:gn * 128],
                                  in_=s_woT[ck * 128:(ck + 1) * 128,
                                            oc0 * 128:(oc0 + gn) * 128]),
                              "s_w", 16)
                        w.wait("tensor", "s_w")
                        for oi in range(gn):
                            w.inc("tensor", lambda e, ck=ck, oi=oi, buf=buf:
                                  e.matmul(
                                      ps_big[:, oi * PT:(oi + 1) * PT],
                                      buf[:, oi * 128:(oi + 1) * 128],
                                      attn_srt[:, ck * PT:(ck + 1) * PT],
                                      start=(ck == 0), stop=(ck == 15)),
                                  "s_mm", 1)
                    w.wait("scalar", "s_mm")
                    for oi in range(gn):
                        oc = oc0 + oi
                        w.wait("scalar", "s_st", w.count["s_st"])
                        w.inc("scalar", lambda e, oi=oi, oc=oc: e.activation(
                            out=sbf_s[:],
                            in_=ps_big[:, oi * PT:(oi + 1) * PT],
                            func=AF.Identity, bias=bo_sb[:, oc:oc + 1]),
                            "s_s", 1)
                        w.wait("sync", "s_s")
                        w.inc("sync", lambda e, p=p, oc=oc: e.dma_start(
                            out=s_sat[p, oc], in_=sbf_s[:]), "s_st", 16)
                    w.wait("tensor", "s_st", w.count["s_st"])

            w.wait("sync", "s_st")
            w.wait("sync", "s_ld")
            w.wait("sync", "s_w")

            with nc.Block() as blk:
                w.run_block(blk)
            counts = dict(w.count)

        with ExitStack() as ctxb:
            bsb = lambda n, s, d: ctxb.enter_context(nc.sbuf_tensor(n, s, d))
            xg2 = bsb("b_xg", [128, HID], F32)
            out2 = bsb("b_out2", [128, HID], F32)
            sx2 = bsb("b_sx", [128, 1], F32)
            sxx2 = bsb("b_sxx", [128, 1], F32)
            lt12 = bsb("b_lt1", [128, 1], F32)
            lt22 = bsb("b_lt2", [128, 1], F32)
            lrs2 = bsb("b_lrs", [128, 1], F32)
            lit2 = bsb("b_lit", [128, 1], I32)
            xn_tm2 = bsb("b_xntm", [128, HID], BF16)
            xn_fm2 = bsb("b_xnfm", [128, 16 * PT], BF16)
            wbuf2 = [bsb(f"b_wbuf{i}", [128, 768], BF16) for i in range(2)]
            gsil = bsb("b_gsil", [128, 6 * PT], BF16)
            ubuf = bsb("b_ubuf", [128, 6 * PT], BF16)
            fuse = bsb("b_fuse", [128, 48 * PT], BF16)
            g_fm = bsb("b_gfm", [128, 16 * PT], BF16)
            s_chunk = bsb("b_schunk", [128, PT], F32)
            sg_c = bsb("b_sgc", [128, PT], F32)
            out_sb = [bsb(f"b_osb{i}", [128, HID], F32) for i in range(4)]

            ps_big = ctxb.enter_context(
                nc.psum_tensor("b_psbig", [128, 6 * PT], F32))
            ps_trb = ctxb.enter_context(
                nc.psum_tensor("b_pstrb", [128, 128], F32))
            ps_trf = ctxb.enter_context(
                nc.psum_tensor("b_pstrf", [128, 128], F32))

            w = Walk(nc, sems, counts)
            DGROUPS = [(0, 6), (6, 6), (12, 4)]

            def ln_block2(x_tile, xn_tile, sq_scratch):
                w.bare("scalar", lambda e: e.activation(
                    out=sq_scratch[:], in_=x_tile[:], func=AF.Square,
                    accum_out=sxx2[:]))
                w.bare("scalar", lambda e: e.activation(
                    out=sq_scratch[:], in_=x_tile[:], func=AF.Identity,
                    accum_out=sx2[:]))
                w.inc_last("scalar", "s_s", 1)
                w.wait("vector", "s_s")
                w.bare("vector", lambda e: e.tensor_scalar(
                    out=lt12[:], in0=sx2[:], scalar1=1.0 / 2048.0,
                    scalar2=None, op0=A.mult))
                pad2()
                w.bare("vector", lambda e: e.tensor_tensor(
                    out=lt22[:], in0=lt12[:], in1=lt12[:], op=A.mult))
                pad2()
                w.bare("vector", lambda e: e.scalar_tensor_tensor(
                    out=lt22[:], in0=sxx2[:], scalar=1.0 / 2048.0,
                    in1=lt22[:], op0=A.mult, op1=A.subtract))
                pad2()
                w.bare("vector", lambda e: e.tensor_scalar(
                    out=lt22[:], in0=lt22[:], scalar1=1e-5, scalar2=None,
                    op0=A.add))
                pad2()
                _rsqrt_newton(w, lt22, lrs2, lt12, lit2, 128, 1, pad=pad2)
                w.bare("vector", lambda e: e.tensor_scalar(
                    out=lt12[:], in0=sx2[:], scalar1=-1.0 / 2048.0,
                    scalar2=None, op0=A.mult))
                pad2()
                w.bare("vector", lambda e: e.tensor_tensor(
                    out=lt12[:], in0=lt12[:], in1=lrs2[:], op=A.mult))
                pad2()
                w.bare("vector", lambda e: e.tensor_scalar(
                    out=xn_tile[:], in0=x_tile[:], scalar1=lrs2[:, 0:1],
                    scalar2=lt12[:, 0:1], op0=A.mult, op1=A.add))

            for p in range(NPASS):
                bidx = p // 4
                r0 = (p % 4) * PT

                for g in range(4):
                    w.wait("sync", "s_v")
                    w.wait("sync", "s_s", w.count["s_s"])
                    w.wait("sync", "s_st", w.count["s_st"])
                    w.inc("sync", lambda e, b=bidx, rr=r0 + g * 128:
                          e.dma_start(out=xg2[:], in_=d_xf[b, rr:rr + 128, :]),
                          "s_ld", 16)
                    w.wait("vector", "s_ld")
                    w.wait("scalar", "s_ld", w.count["s_ld"])
                    ln_block2(xg2, xn_tm2, out2)
                    w.inc_last("vector", "s_v", 1)
                    w.wait("tensor", "s_v")
                    for ck in range(16):
                        w.inc("tensor", lambda e, ck=ck: e.matmul(
                            ps_trb[:], xn_tm2[:, ck * 128:(ck + 1) * 128],
                            iden_b[:], start=True, stop=True), "s_mm", 1)
                        w.wait("vector", "s_mm")
                        w.inc("vector", lambda e, ck=ck, g=g: e.tensor_copy(
                            out=xn_fm2[:, ck * PT + g * 128:
                                       ck * PT + g * 128 + 128],
                            in_=ps_trb[:]), "s_v", 1)
                        w.wait("tensor", "s_v")

                for cg in range(8):
                    for which, src_sc in (("g", s_gq), ("u", s_uq)):
                        for ck in range(16):
                            buf = wbuf2[ck % 2]
                            w.wait("sync", "s_mm")
                            w.inc("sync", lambda e, ck=ck, cg=cg, buf=buf,
                                  src=src_sc: e.dma_start(
                                      out=buf[:],
                                      in_=src[ck * 128:(ck + 1) * 128,
                                              cg * 768:(cg + 1) * 768]),
                                  "s_w", 16)
                            w.wait("tensor", "s_w")
                            for ci in range(6):
                                w.inc("tensor", lambda e, ck=ck, ci=ci, buf=buf:
                                      e.matmul(
                                          ps_big[:, ci * PT:(ci + 1) * PT],
                                          buf[:, ci * 128:(ci + 1) * 128],
                                          xn_fm2[:, ck * PT:(ck + 1) * PT],
                                          start=(ck == 0), stop=(ck == 15)),
                                      "s_mm", 1)
                        w.wait("scalar", "s_mm")
                        w.wait("scalar", "s_v", w.count["s_v"])
                        for ci in range(6):
                            c = cg * 6 + ci
                            dst = gsil if which == "g" else ubuf
                            fn = AF.Silu if which == "g" else AF.Identity
                            sc = gs_sb if which == "g" else us_sb
                            w.inc("scalar", lambda e, ci=ci, c=c, dst=dst,
                                  fn=fn, sc=sc: e.activation(
                                      out=dst[:, ci * PT:(ci + 1) * PT],
                                      in_=ps_big[:, ci * PT:(ci + 1) * PT],
                                      func=fn, scale=sc[:, c:c + 1]), "s_s", 1)
                        w.wait("tensor", "s_s", w.count["s_s"])
                    w.wait("vector", "s_s", w.count["s_s"])
                    w.inc("vector", lambda e, cg=cg: e.tensor_tensor(
                        out=fuse[:, cg * 6 * PT:(cg + 1) * 6 * PT],
                        in0=gsil[:], in1=ubuf[:], op=A.mult), "s_v", 1)

                w.wait("tensor", "s_v", w.count["s_v"])
                for (oc0, gn) in DGROUPS:
                    for c in range(48):
                        buf = wbuf2[c % 2]
                        w.wait("sync", "s_mm")
                        w.inc("sync", lambda e, c=c, oc0=oc0, gn=gn, buf=buf:
                              e.dma_start(
                                  out=buf[:, 0:gn * 128],
                                  in_=s_dq[c * 128:(c + 1) * 128,
                                           oc0 * 128:(oc0 + gn) * 128]),
                              "s_w", 16)
                        w.wait("tensor", "s_w")
                        for oi in range(gn):
                            w.inc("tensor", lambda e, c=c, oi=oi, buf=buf:
                                  e.matmul(
                                      ps_big[:, oi * PT:(oi + 1) * PT],
                                      buf[:, oi * 128:(oi + 1) * 128],
                                      fuse[:, c * PT:(c + 1) * PT],
                                      start=(c == 0), stop=(c == 47)),
                                  "s_mm", 1)
                    w.wait("scalar", "s_mm")
                    for oi in range(gn):
                        oc = oc0 + oi
                        w.inc("scalar", lambda e, oi=oi, oc=oc: e.activation(
                            out=g_fm[:, oc * PT:(oc + 1) * PT],
                            in_=ps_big[:, oi * PT:(oi + 1) * PT],
                            func=AF.Silu, scale=ds_sb[:, oc:oc + 1]), "s_s", 1)
                    w.wait("tensor", "s_s", w.count["s_s"])

                w.wait("vector", "s_s", w.count["s_s"])
                for oc in range(16):
                    w.wait("sync", "s_v", w.count["s_v"])
                    w.inc("sync", lambda e, p=p, oc=oc: e.dma_start(
                        out=s_chunk[:], in_=s_sat[p, oc]), "s_ld", 16)
                    w.wait("vector", "s_ld")
                    w.inc("vector", lambda e, oc=oc: e.tensor_tensor(
                        out=sg_c[:], in0=s_chunk[:],
                        in1=g_fm[:, oc * PT:(oc + 1) * PT], op=A.mult),
                        "s_v", 1)
                    w.wait("tensor", "s_v")
                    for q in range(4):
                        w.inc("tensor", lambda e, q=q: e.transpose(
                            ps_trf[:], sg_c[:, q * 128:(q + 1) * 128],
                            iden_f[:]), "s_mm", 1)
                        w.wait("vector", "s_mm")
                        w.inc("vector", lambda e, q=q, oc=oc: e.tensor_copy(
                            out=out_sb[q][:, oc * 128:(oc + 1) * 128],
                            in_=ps_trf[:]), "s_v", 1)
                        w.wait("tensor", "s_v")
                for q in range(4):
                    w.wait("sync", "s_v", w.count["s_v"])
                    w.wait("sync", "s_st", w.count["s_st"])
                    w.inc("sync", lambda e, b=bidx, rr=r0 + q * 128:
                          e.dma_start(out=xg2[:], in_=d_xf[b, rr:rr + 128, :]),
                          "s_ld", 16)
                    w.wait("vector", "s_ld")
                    w.wait("vector", "s_v", w.count["s_v"])
                    w.inc("vector", lambda e, q=q: e.tensor_tensor(
                        out=out2[:], in0=out_sb[q][:], in1=xg2[:], op=A.add),
                        "s_v", 1)
                    w.wait("sync", "s_v")
                    w.inc("sync", lambda e, b=bidx, rr=r0 + q * 128:
                          e.dma_start(out=d_out[b, rr:rr + 128, :],
                                      in_=out2[:]), "s_st", 16)
                    w.wait("vector", "s_st", w.count["s_st"])

            w.wait("sync", "s_st")
            w.wait("sync", "s_ld")
            w.wait("sync", "s_w")

            with nc.Block() as blk:
                w.run_block(blk)

    return nc


def _make_tables():
    import jax
    import jax.numpy as jnp
    cpu = jax.devices("cpu")[0]
    with jax.default_device(cpu):
        inv = 1.0 / 10000.0 ** (jnp.arange(0, D, 2, dtype=jnp.float32) / D)
        freqs = jnp.arange(S, dtype=jnp.float32)[:, None] * inv[None, :]
        cos = np.asarray(jnp.cos(freqs))
        sin = np.asarray(jnp.sin(freqs))
    return cos, sin


_NC_CACHE = []


def kernel(L0_attn_hidden_states, L0_attn_Wq, L0_attn_bq, L0_attn_Wk,
           L0_attn_bk, L0_attn_Wv, L0_attn_bv, L0_attn_Wo, L0_attn_bo,
           L0_attn_rand_matrix, L0_ffn_gate_w, L0_ffn_up_w, L0_ffn_down_w,
           L0_ffn_gate_scale, L0_ffn_up_scale, L0_ffn_down_scale):
    x = np.ascontiguousarray(np.asarray(L0_attn_hidden_states, np.float32))
    wq = np.asarray(L0_attn_Wq, np.float32)
    wk = np.asarray(L0_attn_Wk, np.float32)
    wv = np.asarray(L0_attn_Wv, np.float32)
    wo = np.asarray(L0_attn_Wo, np.float32)
    wqkvT = np.ascontiguousarray(np.concatenate([wq.T, wk.T, wv.T], axis=1))
    woT = np.ascontiguousarray(wo.T)
    cos, sin = _make_tables()

    maskA = np.zeros((128, 256), np.float32)
    for i in range(16):
        maskA[i * 8:(i + 1) * 8, i * 16:(i + 1) * 16] = 1.0 / 64.0
    iden = np.eye(128, dtype=np.float32)
    sel = np.zeros((128, 24 * 32), np.float32)
    for m in range(24):
        sel[:, m * 32 + (m % 32)] = 1.0
    rT = np.ascontiguousarray(np.asarray(L0_attn_rand_matrix, np.float32).T)

    def fold(vec, n):
        return np.ascontiguousarray(np.asarray(vec, np.float32).reshape(n, 128).T)

    common = {
        "wqkvT": wqkvT, "woT": woT,
        "gw": np.ascontiguousarray(np.asarray(L0_ffn_gate_w, np.float32)),
        "uw": np.ascontiguousarray(np.asarray(L0_ffn_up_w, np.float32)),
        "dw": np.ascontiguousarray(np.asarray(L0_ffn_down_w, np.float32)),
        "rT": rT, "maskA": maskA, "iden": iden, "sel": sel,
        "bq": fold(L0_attn_bq, 16), "bk": fold(L0_attn_bk, 8),
        "bv": fold(L0_attn_bv, 8), "bo": fold(L0_attn_bo, 16),
        "gs": fold(L0_ffn_gate_scale, 48), "us": fold(L0_ffn_up_scale, 48),
        "ds": fold(L0_ffn_down_scale, 16),
    }

    in_maps = []
    for c in range(NCORES):
        s0 = c * SP
        xa = np.ascontiguousarray(x[:, s0:s0 + SP, :])
        taus = np.arange(TP) + c * TP
        rows = (np.arange(H)[None, :] * 1024 + taus[:, None]).reshape(-1)
        xf = np.ascontiguousarray(x[:, rows, :])
        cseg = cos[s0:s0 + SP, :]
        sseg = sin[s0:s0 + SP, :]
        cosf = np.ascontiguousarray(np.concatenate([cseg.T, cseg.T], axis=0))
        ssinf = np.ascontiguousarray(np.concatenate([-sseg.T, sseg.T], axis=0))
        m = dict(common)
        m.update({"xa": xa, "xf": xf, "cosf": cosf, "ssinf": ssinf})
        in_maps.append(m)

    if not _NC_CACHE:
        _NC_CACHE.append(build_program())
    nc = _NC_CACHE[0]
    res = run_bass_kernel_spmd(nc, in_maps, core_ids=list(range(NCORES)))

    out = np.empty((B, S, HID), np.float32)
    for c in range(NCORES):
        o = np.asarray(res.results[c]["out"])
        taus = np.arange(TP) + c * TP
        rows = (np.arange(H)[None, :] * 1024 + taus[:, None]).reshape(-1)
        out[:, rows, :] = o
    return out


# version 09
# speedup vs baseline: 1.3304x; 1.3304x over previous
import os as _os
_v = _os.environ.get("JAX_PLATFORMS", "")
if _v and "cpu" not in _v.split(","):
    _os.environ["JAX_PLATFORMS"] = _v + ",cpu"

import numpy as np
from contextlib import ExitStack

import concourse.bass as bass
import concourse.mybir as mybir
from concourse.bass_utils import run_bass_kernel_spmd
from concourse.alu_op_type import AluOpType as A

F32 = mybir.dt.float32
I32 = mybir.dt.int32
BF16 = mybir.dt.bfloat16
AF = mybir.ActivationFunctionType

NCORES = 8
B, S, HID = 2, 16384, 2048
H, HKV, D = 16, 8, 128
FFI = 6144
SP = S // NCORES
TP = SP // 16
PT = 512
NPASS = 8

PI_LO = float(np.nextafter(np.float32(np.pi), np.float32(0.0)))
TWO_PI = 2.0 * np.pi
C1 = float(np.float32(TWO_PI))
C2 = float(np.float32(np.float64(TWO_PI) - np.float64(np.float32(TWO_PI))))
HALF_PI = float(np.float32(np.pi / 2))
HPI_ADJ = float(np.float32(np.float64(np.float32(np.pi / 2))
                           - np.float64(np.pi) / 2))


class Walk:

    ENGINES = ("sync", "vector", "scalar", "tensor", "gpsimd")

    def __init__(self, nc, sems, counts=None):
        self.nc = nc
        self.prog = {e: [] for e in self.ENGINES}
        self.sems = sems
        self.count = dict(counts) if counts else {k: 0 for k in sems}

    def emit(self, eng, fn):
        self.prog[eng].append(fn)

    def inc(self, eng, mk, sem, n):
        s = self.sems[sem]
        self.emit(eng, lambda e, mk=mk, s=s, n=n: mk(e).then_inc(s, n))
        self.count[sem] += n

    def bare(self, eng, mk):
        self.emit(eng, lambda e, mk=mk: mk(e))

    def inc_last(self, eng, sem, n):
        s = self.sems[sem]
        old = self.prog[eng].pop()
        self.emit(eng, lambda e, old=old, s=s, n=n: old(e).then_inc(s, n))
        self.count[sem] += n

    def wait(self, eng, sem, val=None):
        s = self.sems[sem]
        v = self.count[sem] if val is None else val
        if v > 0:
            self.emit(eng, lambda e, s=s, v=v: e.wait_ge(s, v))

    def run_block(self, block):
        for name in self.ENGINES:
            closures = self.prog[name]
            if not closures:
                continue

            def body(eng, closures=closures):
                for fn in closures:
                    fn(eng)

            getattr(block, name)(body)


def _rsqrt_newton(w, src, dst, scr, itile, rows, cols, pad=None):
    if pad is None:
        pad = lambda: None
    c = (slice(0, rows), slice(0, cols))
    w.bare("vector", lambda e: e.tensor_scalar(
        out=itile[c], in0=src[c].bitcast(I32), scalar1=1, scalar2=None,
        op0=A.logical_shift_right))
    pad()
    w.bare("vector", lambda e: e.tensor_scalar(
        out=itile[c], in0=itile[c], scalar1=-1, scalar2=float(0x5f3759df),
        op0=A.mult, op1=A.add))
    pad()
    w.bare("vector", lambda e: e.tensor_scalar(
        out=dst[c], in0=itile[c].bitcast(F32), scalar1=1.0, scalar2=None,
        op0=A.mult))
    pad()
    for _ in range(4):
        w.bare("vector", lambda e: e.tensor_tensor(
            out=scr[c], in0=src[c], in1=dst[c], op=A.mult))
        pad()
        w.bare("vector", lambda e: e.tensor_tensor(
            out=scr[c], in0=scr[c], in1=dst[c], op=A.mult))
        pad()
        w.bare("vector", lambda e: e.tensor_scalar(
            out=scr[c], in0=scr[c], scalar1=-0.5, scalar2=1.5,
            op0=A.mult, op1=A.add))
        pad()
        w.bare("vector", lambda e: e.tensor_tensor(
            out=dst[c], in0=dst[c], in1=scr[c], op=A.mult))
        pad()


def _quant_block(w, wf, it1, it2, it3, ft1, ft2, ft3, ft4, ft5, mki, out_bf,
                 ncols):
    sl = (slice(None), slice(0, ncols))
    bits = wf[sl].bitcast(I32)
    w.bare("vector", lambda e: e.tensor_scalar(
        out=it1[sl], in0=bits, scalar1=0x7FFFFF, scalar2=None, op0=A.bitwise_and))
    w.bare("vector", lambda e: e.tensor_scalar(
        out=it1[sl], in0=it1[sl], scalar1=float(0x7FFFF), scalar2=None, op0=A.add))
    w.bare("vector", lambda e: e.tensor_scalar(
        out=it2[sl], in0=bits, scalar1=20, scalar2=1,
        op0=A.logical_shift_right, op1=A.bitwise_and))
    w.bare("vector", lambda e: e.tensor_tensor(
        out=it1[sl], in0=it1[sl], in1=it2[sl], op=A.add))
    w.bare("vector", lambda e: e.tensor_scalar(
        out=it1[sl], in0=it1[sl], scalar1=20, scalar2=20,
        op0=A.logical_shift_right, op1=A.logical_shift_left))
    w.bare("vector", lambda e: e.tensor_scalar(
        out=it3[sl], in0=bits, scalar1=23, scalar2=23,
        op0=A.logical_shift_right, op1=A.logical_shift_left))
    w.bare("vector", lambda e: e.tensor_tensor(
        out=it3[sl], in0=it3[sl], in1=it1[sl], op=A.add))
    w.bare("vector", lambda e: e.tensor_copy(out=mki[sl], in_=ft1[sl]))
    w.bare("vector", lambda e: e.tensor_copy(out=ft2[sl], in_=mki[sl]))
    w.bare("vector", lambda e: e.tensor_tensor(
        out=ft3[sl], in0=ft1[sl], in1=ft2[sl], op=A.subtract))
    w.bare("vector", lambda e: e.tensor_scalar(
        out=ft4[sl], in0=ft3[sl], scalar1=0.5, scalar2=None, op0=A.is_gt))
    w.bare("vector", lambda e: e.tensor_tensor(
        out=ft2[sl], in0=ft2[sl], in1=ft4[sl], op=A.add))
    w.bare("vector", lambda e: e.tensor_scalar(
        out=ft4[sl], in0=ft3[sl], scalar1=0.5, scalar2=None, op0=A.is_equal))
    w.bare("vector", lambda e: e.tensor_scalar(
        out=it2[sl], in0=mki[sl], scalar1=1, scalar2=None, op0=A.bitwise_and))
    w.bare("vector", lambda e: e.tensor_copy(out=ft3[sl], in_=it2[sl]))
    w.bare("vector", lambda e: e.tensor_tensor(
        out=ft4[sl], in0=ft4[sl], in1=ft3[sl], op=A.mult))
    w.bare("vector", lambda e: e.tensor_tensor(
        out=ft2[sl], in0=ft2[sl], in1=ft4[sl], op=A.add))
    w.bare("vector", lambda e: e.tensor_scalar(
        out=ft2[sl], in0=ft2[sl], scalar1=float(2.0 ** -9), scalar2=None,
        op0=A.mult))
    w.bare("vector", lambda e: e.tensor_tensor(
        out=ft2[sl], in0=ft2[sl], in1=ft5[sl], op=A.mult))
    w.bare("vector", lambda e: e.tensor_scalar(
        out=mki[sl], in0=bits, scalar1=23, scalar2=255,
        op0=A.logical_shift_right, op1=A.bitwise_and))
    w.bare("vector", lambda e: e.tensor_scalar(
        out=mki[sl], in0=mki[sl], scalar1=120, scalar2=None, op0=A.is_le))
    w.bare("vector", lambda e: e.select(
        out=out_bf[sl], mask=mki[sl], on_true=ft2[sl],
        on_false=it3[sl].bitcast(F32)))


def build_program():
    nc = bass.Bass("TRN2", target_bir_lowering=False,
                   detect_race_conditions=False)

    d_xa = nc.dram_tensor("xa", [B, SP, HID], F32, kind="ExternalInput").ap()
    d_xf = nc.dram_tensor("xf", [B, SP, HID], F32, kind="ExternalInput").ap()
    d_wqkvT = nc.dram_tensor("wqkvT", [HID, 4096], F32, kind="ExternalInput").ap()
    d_woT = nc.dram_tensor("woT", [HID, HID], F32, kind="ExternalInput").ap()
    d_gw = nc.dram_tensor("gw", [HID, FFI], F32, kind="ExternalInput").ap()
    d_uw = nc.dram_tensor("uw", [HID, FFI], F32, kind="ExternalInput").ap()
    d_dw = nc.dram_tensor("dw", [FFI, HID], F32, kind="ExternalInput").ap()
    d_cosf = nc.dram_tensor("cosf", [128, SP], F32, kind="ExternalInput").ap()
    d_ssinf = nc.dram_tensor("ssinf", [128, SP], F32, kind="ExternalInput").ap()
    d_rT = nc.dram_tensor("rT", [128, 64], F32, kind="ExternalInput").ap()
    d_maskA = nc.dram_tensor("maskA", [128, 256], F32, kind="ExternalInput").ap()
    d_iden = nc.dram_tensor("iden", [128, 128], F32, kind="ExternalInput").ap()
    d_sel = nc.dram_tensor("sel", [128, 24 * 32], F32, kind="ExternalInput").ap()
    d_bq = nc.dram_tensor("bq", [128, 16], F32, kind="ExternalInput").ap()
    d_bk = nc.dram_tensor("bk", [128, 8], F32, kind="ExternalInput").ap()
    d_bv = nc.dram_tensor("bv", [128, 8], F32, kind="ExternalInput").ap()
    d_bo = nc.dram_tensor("bo", [128, 16], F32, kind="ExternalInput").ap()
    d_gs = nc.dram_tensor("gs", [128, 48], F32, kind="ExternalInput").ap()
    d_us = nc.dram_tensor("us", [128, 48], F32, kind="ExternalInput").ap()
    d_ds = nc.dram_tensor("ds", [128, 16], F32, kind="ExternalInput").ap()

    d_out = nc.dram_tensor("out", [B, SP, HID], F32, kind="ExternalOutput").ap()
    s_wqkv = nc.dram_tensor("s_wqkv", [HID, 4096], BF16, kind="Internal").ap()
    s_woT = nc.dram_tensor("s_woT", [HID, HID], BF16, kind="Internal").ap()
    s_gq = nc.dram_tensor("s_gq", [HID, FFI], BF16, kind="Internal").ap()
    s_uq = nc.dram_tensor("s_uq", [HID, FFI], BF16, kind="Internal").ap()
    s_dq = nc.dram_tensor("s_dq", [FFI, HID], BF16, kind="Internal").ap()
    s_sat = nc.dram_tensor("s_sat", [NPASS, 16, 128, PT], F32,
                           kind="Internal").ap()
    s_rsq = nc.dram_tensor("s_rsq", [32, PT], F32, kind="Internal").ap()

    with ExitStack() as ctx:
        sbt = lambda n, s, d: ctx.enter_context(nc.sbuf_tensor(n, s, d))
        cosf = sbt("cosf_t", [128, SP], F32)
        ssinf = sbt("ssinf_t", [128, SP], F32)
        rT_b = sbt("rT_b", [128, 64], BF16)
        maskA_b = sbt("maskA_b", [128, 256], BF16)
        iden_b = sbt("iden_b", [128, 128], BF16)
        iden_f = sbt("iden_f", [128, 128], F32)
        sel_b = sbt("sel_b", [128, 24 * 32], BF16)
        bq_sb = sbt("bq_sb", [128, 16], F32)
        bk_sb = sbt("bk_sb", [128, 8], F32)
        bv_sb = sbt("bv_sb", [128, 8], F32)
        bo_sb = sbt("bo_sb", [128, 16], F32)
        gs_sb = sbt("gs_sb", [128, 48], F32)
        us_sb = sbt("us_sb", [128, 48], F32)
        ds_sb = sbt("ds_sb", [128, 16], F32)
        cst_hpi = sbt("cst_hpi", [128, 1], F32)
        cst_adj = sbt("cst_adj", [128, 1], F32)
        fv = sbt("fence_v", [1, 8], F32)
        nc.const_aps.register(F32, HALF_PI, cst_hpi[:, 0:1])
        nc.const_aps.register(F32, HPI_ADJ, cst_adj[:, 0:1])

        sems = {k: ctx.enter_context(nc.semaphore(k)) for k in
                ("s_ld", "s_w", "s_mm", "s_v", "s_s", "s_st", "s_g", "s_c")}

        def pad2():
            for _ in range(2):
                w.bare("vector", lambda e: e.tensor_scalar(
                    out=fv[0:1, 0:4], in0=fv[0:1, 4:8], scalar1=0.0,
                    scalar2=None, op0=A.mult))

        with ExitStack() as ctxp:
            psb = lambda n, s, d: ctxp.enter_context(nc.sbuf_tensor(n, s, d))
            QW = 3072
            wf = psb("p_wf", [128, QW], F32)
            wb = psb("p_wb", [128, QW], BF16)
            it1 = psb("p_it1", [128, QW], I32)
            it2 = psb("p_it2", [128, QW], I32)
            it3 = psb("p_it3", [128, QW], I32)
            ft1 = psb("p_ft1", [128, QW], F32)
            ft2 = psb("p_ft2", [128, QW], F32)
            ft3 = psb("p_ft3", [128, QW], F32)
            ft4 = psb("p_ft4", [128, QW], F32)
            ft5 = psb("p_ft5", [128, QW], F32)
            mki = psb("p_mki", [128, QW], I32)
            cb = psb("p_cb", [128, 4096], BF16)

            w = Walk(nc, sems)
            w.inc("vector", lambda e: e.memset(cst_hpi[:], HALF_PI), "s_c", 1)
            w.inc("vector", lambda e: e.memset(cst_adj[:], HPI_ADJ), "s_c", 1)
            w.wait("scalar", "s_c")
            w.bare("vector", lambda e: e.memset(fv[:], 0.0))
            for dst, src in ((cosf, d_cosf), (ssinf, d_ssinf),
                             (bq_sb, d_bq), (bk_sb, d_bk), (bv_sb, d_bv),
                             (bo_sb, d_bo), (gs_sb, d_gs), (us_sb, d_us),
                             (ds_sb, d_ds), (iden_f, d_iden)):
                w.inc("sync", lambda e, dst=dst, src=src: e.dma_start(
                    out=dst[:], in_=src[:]), "s_ld", 16)
            for dst, src in ((rT_b, d_rT), (maskA_b, d_maskA),
                             (iden_b, d_iden), (sel_b, d_sel)):
                w.inc("gpsimd", lambda e, dst=dst, src=src: e.dma_start(
                    out=dst[:], in_=src[:]), "s_g", 16)

            def conv_stream(src_ap, dst_ap, nblk, width):
                for i in range(nblk):
                    w.inc("gpsimd", lambda e, i=i, src=src_ap, wd=width:
                          e.dma_start(out=cb[:, 0:wd],
                                      in_=src[i * 128:(i + 1) * 128, :]),
                          "s_g", 16)
                    w.wait("sync", "s_g")
                    w.inc("sync", lambda e, i=i, dst=dst_ap, wd=width:
                          e.dma_start(out=dst[i * 128:(i + 1) * 128, :],
                                      in_=cb[:, 0:wd]),
                          "s_st", 16)
                    w.wait("gpsimd", "s_st")

            conv_stream(d_wqkvT, s_wqkv, 16, 4096)
            conv_stream(d_woT, s_woT, 16, 2048)

            def quant_stream(src_ap, dst_ap, nblk, tot_width):
                nchunk = (tot_width + 3071) // 3072
                for i0 in range(nblk * nchunk):
                    i, cw = i0 // nchunk, i0 % nchunk
                    width = min(3072, tot_width - cw * 3072)
                    c0 = cw * 3072
                    w.inc("sync", lambda e, i=i, c0=c0, src=src_ap, wd=width:
                          e.dma_start(out=wf[:, 0:wd],
                                      in_=src[i * 128:(i + 1) * 128,
                                              c0:c0 + wd]),
                          "s_ld", 16)
                    w.wait("scalar", "s_ld")
                    w.bare("scalar", lambda e, wd=width: e.activation(
                        out=ft1[:, 0:wd], in_=wf[:, 0:wd], func=AF.Abs,
                        scale=512.0))
                    w.bare("scalar", lambda e, wd=width: e.activation(
                        out=ft5[:, 0:wd], in_=wf[:, 0:wd], func=AF.Sign))
                    w.inc_last("scalar", "s_s", 1)
                    w.wait("vector", "s_s")
                    w.wait("vector", "s_ld")
                    _quant_block(w, wf, it1, it2, it3, ft1, ft2, ft3, ft4,
                                 ft5, mki, wb, width)
                    w.inc_last("vector", "s_v", 1)
                    w.wait("sync", "s_v")
                    w.inc("sync", lambda e, i=i, c0=c0, dst=dst_ap, wd=width:
                          e.dma_start(out=dst[i * 128:(i + 1) * 128,
                                              c0:c0 + wd],
                                      in_=wb[:, 0:wd]),
                          "s_st", 16)
                    w.wait("sync", "s_st")
                    w.wait("scalar", "s_st", w.count["s_st"])
                    w.wait("vector", "s_st", w.count["s_st"])

            quant_stream(d_gw, s_gq, 16, FFI)
            quant_stream(d_uw, s_uq, 16, FFI)
            quant_stream(d_dw, s_dq, 48, HID)
            w.wait("sync", "s_st")
            w.wait("sync", "s_ld")
            w.wait("gpsimd", "s_g")

            with nc.Block() as blk:
                w.run_block(blk)
            counts = dict(w.count)

        with ExitStack() as ctxa:
            asb = lambda n, s, d: ctxa.enter_context(nc.sbuf_tensor(n, s, d))
            xg = asb("a_xg", [128, HID], F32)
            xsq = asb("a_xsq", [128, HID], F32)
            sx = asb("a_sx", [128, 1], F32)
            sxx = asb("a_sxx", [128, 1], F32)
            lt1 = asb("a_lt1", [128, 1], F32)
            lt2 = asb("a_lt2", [128, 1], F32)
            lrs = asb("a_lrs", [128, 1], F32)
            lit = asb("a_lit", [128, 1], I32)
            xn_tm = asb("a_xntm", [128, HID], BF16)
            xn_fm = asb("a_xnfm", [128, 16 * PT], BF16)
            wbuf = [asb(f"a_wbuf{i}", [128, 640], BF16) for i in range(2)]
            qa = asb("a_qa", [128, PT], F32)
            qb = asb("a_qb", [128, PT], F32)
            t1 = asb("a_t1", [128, PT], F32)
            t2 = asb("a_t2", [128, PT], F32)
            sqb = asb("a_sqb", [128, PT], BF16)
            qrot = asb("a_qrot", [128, 24 * PT], BF16)
            v_il = asb("a_vil", [128, 8 * PT], BF16)
            ssqf = asb("a_ssqf", [32, PT], F32)
            rsq = asb("a_rsq", [32, PT], F32)
            nsc = asb("a_nsc", [32, PT], F32)
            nit = asb("a_nit", [32, PT], I32)
            qhat = asb("a_qhat", [128, PT], BF16)
            ppail = asb("a_pp", [128, PT], F32)
            tg1 = asb("a_tg1", [128, PT], F32)
            tg2 = asb("a_tg2", [128, PT], F32)
            tga = asb("a_tga", [64, PT], F32)
            tga2 = asb("a_tga2", [64, PT], F32)
            qp_il = asb("a_qpil", [128, 16 * PT], BF16)
            kp_il = asb("a_kpil", [128, 8 * PT], BF16)
            abuf = asb("a_abuf", [128, 256], BF16)
            bcb = asb("a_bcb", [128, PT], F32)
            vT = asb("a_vT", [128, 128], BF16)
            attn_srt = asb("a_srt", [128, 16 * PT], BF16)
            sbf_s = asb("a_sbf", [128, PT], F32)

            ps_big = ctxa.enter_context(
                nc.psum_tensor("a_psbig", [128, 5 * PT], F32))
            ps_ssq = ctxa.enter_context(
                nc.psum_tensor("a_psssq", [32, PT], F32))
            ps_aux = ctxa.enter_context(
                nc.psum_tensor("a_psaux", [128, PT], F32))
            ps_trb = ctxa.enter_context(
                nc.psum_tensor("a_pstrb", [128, 128], F32))

            w = Walk(nc, sems, counts)

            QKV_GROUPS = [(0, 5), (5, 5), (10, 5), (15, 5), (20, 5), (25, 5),
                          (30, 2)]
            WO_GROUPS = [(0, 5), (5, 5), (10, 5), (15, 1)]

            def ln_block(x_tile, xn_tile, sq_scratch):
                w.bare("scalar", lambda e: e.activation(
                    out=sq_scratch[:], in_=x_tile[:], func=AF.Square,
                    accum_out=sxx[:]))
                w.bare("scalar", lambda e: e.activation(
                    out=sq_scratch[:], in_=x_tile[:], func=AF.Identity,
                    accum_out=sx[:]))
                w.inc_last("scalar", "s_s", 1)
                w.wait("vector", "s_s")
                w.bare("vector", lambda e: e.tensor_scalar(
                    out=lt1[:], in0=sx[:], scalar1=1.0 / 2048.0, scalar2=None,
                    op0=A.mult))
                pad2()
                w.bare("vector", lambda e: e.tensor_tensor(
                    out=lt2[:], in0=lt1[:], in1=lt1[:], op=A.mult))
                pad2()
                w.bare("vector", lambda e: e.scalar_tensor_tensor(
                    out=lt2[:], in0=sxx[:], scalar=1.0 / 2048.0, in1=lt2[:],
                    op0=A.mult, op1=A.subtract))
                pad2()
                w.bare("vector", lambda e: e.tensor_scalar(
                    out=lt2[:], in0=lt2[:], scalar1=1e-5, scalar2=None,
                    op0=A.add))
                pad2()
                _rsqrt_newton(w, lt2, lrs, lt1, lit, 128, 1, pad=pad2)
                w.bare("vector", lambda e: e.tensor_scalar(
                    out=lt1[:], in0=sx[:], scalar1=-1.0 / 2048.0, scalar2=None,
                    op0=A.mult))
                pad2()
                w.bare("vector", lambda e: e.tensor_tensor(
                    out=lt1[:], in0=lt1[:], in1=lrs[:], op=A.mult))
                pad2()
                w.bare("vector", lambda e: e.tensor_scalar(
                    out=xn_tile[:], in0=x_tile[:], scalar1=lrs[:, 0:1],
                    scalar2=lt1[:, 0:1], op0=A.mult, op1=A.add))

            for p in range(NPASS):
                bidx = p // 4
                s0 = (p % 4) * PT

                for g in range(4):
                    w.wait("sync", "s_v")
                    w.wait("sync", "s_s", w.count["s_s"])
                    w.inc("sync", lambda e, b=bidx, r=s0 + g * 128:
                          e.dma_start(out=xg[:], in_=d_xa[b, r:r + 128, :]),
                          "s_ld", 16)
                    w.wait("vector", "s_ld")
                    w.wait("scalar", "s_ld", w.count["s_ld"])
                    ln_block(xg, xn_tm, xsq)
                    w.inc_last("vector", "s_v", 1)
                    w.wait("tensor", "s_v")
                    for ck in range(16):
                        w.inc("tensor", lambda e, ck=ck: e.matmul(
                            ps_trb[:], xn_tm[:, ck * 128:(ck + 1) * 128],
                            iden_b[:], start=True, stop=True), "s_mm", 1)
                        w.wait("vector", "s_mm")
                        w.inc("vector", lambda e, ck=ck, g=g: e.tensor_copy(
                            out=xn_fm[:, ck * PT + g * 128:ck * PT + g * 128 + 128],
                            in_=ps_trb[:]), "s_v", 1)
                        w.wait("tensor", "s_v")

                nqk = 0
                for (oc0, gn) in QKV_GROUPS:
                    for ck in range(16):
                        buf = wbuf[ck % 2]
                        eng = "sync" if ck % 2 == 0 else "gpsimd"
                        sem = "s_w" if ck % 2 == 0 else "s_g"
                        w.wait(eng, "s_mm",
                               w.count["s_mm"] - (gn if ck >= 2 else 0))
                        w.inc(eng, lambda e, ck=ck, oc0=oc0, gn=gn, buf=buf:
                              e.dma_start(
                                  out=buf[:, 0:gn * 128],
                                  in_=s_wqkv[ck * 128:(ck + 1) * 128,
                                             oc0 * 128:(oc0 + gn) * 128]),
                              sem, 16)
                        w.wait("tensor", sem)
                        for oi in range(gn):
                            w.inc("tensor", lambda e, ck=ck, oi=oi, buf=buf:
                                  e.matmul(
                                      ps_big[:, oi * PT:(oi + 1) * PT],
                                      buf[:, oi * 128:(oi + 1) * 128],
                                      xn_fm[:, ck * PT:(ck + 1) * PT],
                                      start=(ck == 0), stop=(ck == 15)),
                                  "s_mm", 1)
                    w.wait("scalar", "s_mm")
                    w.wait("vector", "s_mm", w.count["s_mm"])
                    for oi in range(gn):
                        oc = oc0 + oi
                        pslice = ps_big[:, oi * PT:(oi + 1) * PT]
                        if oc < 24:
                            bias_ap = (bq_sb[:, oc:oc + 1] if oc < 16
                                       else bk_sb[:, oc - 16:oc - 15])
                            w.wait("scalar", "s_v", w.count["s_v"])
                            w.bare("scalar", lambda e, ps=pslice, ba=bias_ap:
                                   e.activation(out=qa[:], in_=ps,
                                                func=AF.Identity, bias=ba))
                            w.inc_last("scalar", "s_s", 1)
                            w.wait("vector", "s_s")
                            w.bare("vector", lambda e: e.tensor_scalar(
                                out=qb[0:64, :], in0=qa[64:128, :],
                                scalar1=1.0, scalar2=None, op0=A.mult))
                            w.bare("vector", lambda e: e.tensor_scalar(
                                out=qb[64:128, :], in0=qa[0:64, :],
                                scalar1=1.0, scalar2=None, op0=A.mult))
                            cs = cosf[:, s0:s0 + PT]
                            ss = ssinf[:, s0:s0 + PT]
                            w.bare("vector", lambda e, cs=cs: e.tensor_tensor(
                                out=t1[:], in0=qa[:], in1=cs, op=A.mult))
                            w.bare("vector", lambda e, ss=ss: e.tensor_tensor(
                                out=t2[:], in0=qb[:], in1=ss, op=A.mult))
                            w.bare("vector", lambda e: e.tensor_tensor(
                                out=t1[:], in0=t1[:], in1=t2[:], op=A.add))
                            w.bare("vector", lambda e, oc=oc: e.tensor_copy(
                                out=qrot[:, oc * PT:(oc + 1) * PT], in_=t1[:]))
                            w.bare("vector", lambda e: e.tensor_tensor(
                                out=sqb[:], in0=t1[:], in1=t1[:], op=A.mult))
                            w.inc_last("vector", "s_v", 1)
                            w.wait("tensor", "s_v")
                            w.inc("tensor", lambda e, k=nqk: e.matmul(
                                ps_ssq[:], sel_b[:, k * 32:(k + 1) * 32],
                                sqb[:], start=(k == 0), stop=(k == 23)),
                                "s_mm", 1)
                            nqk += 1
                        else:
                            j = oc - 24
                            w.inc("scalar", lambda e, ps=pslice, j=j:
                                  e.activation(
                                      out=v_il[:].rearrange(
                                          "p (t j) -> p t j", j=8)[:, :, j],
                                      in_=ps, func=AF.Identity,
                                      bias=bv_sb[:, j:j + 1]), "s_s", 1)
                    w.inc("vector", lambda e: e.tensor_scalar(
                        out=fv[0:1, 0:4], in0=fv[0:1, 4:8], scalar1=0.0,
                        scalar2=None, op0=A.mult), "s_v", 1)
                    w.wait("tensor", "s_v")
                    w.wait("tensor", "s_s", w.count["s_s"])

                w.wait("vector", "s_mm", w.count["s_mm"])
                w.bare("vector", lambda e: e.tensor_scalar(
                    out=ssqf[:], in0=ps_ssq[:], scalar1=1e-12, scalar2=None,
                    op0=A.max))
                _rsqrt_newton(w, ssqf, rsq, nsc, nit, 32, PT)
                w.inc_last("vector", "s_v", 1)
                w.wait("sync", "s_v")
                w.inc("sync", lambda e: e.dma_start(out=s_rsq[:], in_=rsq[:]),
                      "s_st", 16)
                w.wait("gpsimd", "s_st", w.count["s_st"])

                for m in range(24):
                    pair, half = m // 2, m % 2
                    w.wait("gpsimd", "s_v", w.count["s_v"])
                    w.inc("gpsimd", lambda e, m=m: e.dma_start(
                        out=bcb[:],
                        in_=s_rsq[m:m + 1, :].broadcast_to((128, PT))),
                        "s_g", 16)
                    w.wait("vector", "s_g")
                    w.inc("vector", lambda e, m=m: e.tensor_tensor(
                        out=qhat[:], in0=qrot[:, m * PT:(m + 1) * PT],
                        in1=bcb[:], op=A.mult), "s_v", 1)
                    w.wait("tensor", "s_v")
                    w.inc("tensor", lambda e, half=half: e.matmul(
                        ps_big[half * 64:(half + 1) * 64,
                               half * PT:(half + 1) * PT],
                        rT_b[:], qhat[:], start=True, stop=True), "s_mm", 1)
                    if half == 1:
                        w.wait("vector", "s_mm")
                        w.bare("vector", lambda e: e.tensor_scalar(
                            out=ppail[0:64, :], in0=ps_big[0:64, 0:PT],
                            scalar1=1.0, scalar2=None, op0=A.mult))
                        w.inc("vector", lambda e: e.tensor_scalar(
                            out=ppail[64:128, :], in0=ps_big[64:128, PT:2 * PT],
                            scalar1=1.0, scalar2=None, op0=A.mult), "s_v", 1)
                        w.wait("tensor", "s_v")
                        w.bare("vector", lambda e: e.tensor_scalar(
                            out=tg1[:], in0=ppail[:], scalar1=PI_LO,
                            scalar2=None, op0=A.is_gt))
                        w.bare("vector", lambda e: e.tensor_scalar(
                            out=tg2[:], in0=ppail[:], scalar1=-PI_LO,
                            scalar2=None, op0=A.is_lt))
                        w.bare("vector", lambda e: e.tensor_tensor(
                            out=tg1[:], in0=tg1[:], in1=tg2[:], op=A.subtract))
                        w.bare("vector", lambda e: e.scalar_tensor_tensor(
                            out=tg2[:], in0=tg1[:], scalar=-C1, in1=ppail[:],
                            op0=A.mult, op1=A.add))
                        w.bare("vector", lambda e: e.scalar_tensor_tensor(
                            out=tg2[:], in0=tg1[:], scalar=-C2, in1=tg2[:],
                            op0=A.mult, op1=A.add))
                        w.bare("vector", lambda e: e.tensor_scalar(
                            out=tg2[:], in0=tg2[:], scalar1=-PI_LO,
                            scalar2=PI_LO, op0=A.max, op1=A.min))
                        w.inc_last("vector", "s_v", 1)
                        w.wait("scalar", "s_v")
                        for hh in range(2):
                            hm = pair * 2 + hh
                            if hm < 16:
                                dstt = qp_il[:].rearrange(
                                    "p (t h) -> p t h", h=16)
                                sin_dst = dstt[0:64, :, hm]
                                cos_dst = dstt[64:128, :, hm]
                            else:
                                dstt = kp_il[:].rearrange(
                                    "p (t j) -> p t j", j=8)
                                sin_dst = dstt[0:64, :, hm - 16]
                                cos_dst = dstt[64:128, :, hm - 16]
                            rrs = tg2[hh * 64:(hh + 1) * 64, :]
                            w.bare("scalar", lambda e, rrs=rrs, o=sin_dst:
                                   e.activation(out=o, in_=rrs, func=AF.Sin))
                            w.bare("scalar", lambda e, rrs=rrs: e.activation(
                                out=tga[:], in_=rrs, func=AF.Abs))
                            w.bare("scalar", lambda e: e.activation(
                                out=tga2[:], in_=tga[:], func=AF.Identity,
                                bias=HPI_ADJ))
                            w.bare("scalar", lambda e, o=cos_dst: e.activation(
                                out=o, in_=tga2[:], func=AF.Sin, scale=-1.0,
                                bias=HALF_PI))
                        w.inc_last("scalar", "s_s", 1)
                        w.wait("vector", "s_s")
                        w.wait("tensor", "s_s", w.count["s_s"])

                for g in range(32):
                    w.inc("tensor", lambda e, g=g: e.matmul(
                        ps_big[:, 0:256], kp_il[:, g * 128:(g + 1) * 128],
                        qp_il[:, g * 256:(g + 1) * 256], start=True,
                        stop=True), "s_mm", 1)
                    w.wait("vector", "s_mm")
                    w.inc("vector", lambda e: e.tensor_tensor(
                        out=abuf[:], in0=ps_big[:, 0:256], in1=maskA_b[:],
                        op=A.mult), "s_v", 1)
                    w.inc("tensor", lambda e, g=g: e.matmul(
                        ps_trb[:], v_il[:, g * 128:(g + 1) * 128],
                        iden_b[:], start=True, stop=True), "s_mm", 1)
                    w.wait("vector", "s_mm")
                    w.inc("vector", lambda e: e.tensor_copy(
                        out=vT[:], in_=ps_trb[:]), "s_v", 1)
                    w.wait("tensor", "s_v")
                    w.inc("tensor", lambda e: e.matmul(
                        ps_aux[:, 0:256], vT[:], abuf[:], start=True,
                        stop=True), "s_mm", 1)
                    w.wait("scalar", "s_mm")
                    srt4 = attn_srt[:].rearrange(
                        "p (ti g2 h) -> p ti g2 h", ti=16, g2=32)
                    w.inc("scalar", lambda e, g=g, d4=srt4: e.activation(
                        out=d4[:, :, g, :], in_=ps_aux[:, 0:256],
                        func=AF.Identity), "s_s", 1)
                    w.wait("tensor", "s_s", w.count["s_s"])

                for (oc0, gn) in WO_GROUPS:
                    for ck in range(16):
                        buf = wbuf[ck % 2]
                        w.wait("sync", "s_mm")
                        w.inc("sync", lambda e, ck=ck, oc0=oc0, gn=gn, buf=buf:
                              e.dma_start(
                                  out=buf[:, 0:gn * 128],
                                  in_=s_woT[ck * 128:(ck + 1) * 128,
                                            oc0 * 128:(oc0 + gn) * 128]),
                              "s_w", 16)
                        w.wait("tensor", "s_w")
                        for oi in range(gn):
                            w.inc("tensor", lambda e, ck=ck, oi=oi, buf=buf:
                                  e.matmul(
                                      ps_big[:, oi * PT:(oi + 1) * PT],
                                      buf[:, oi * 128:(oi + 1) * 128],
                                      attn_srt[:, ck * PT:(ck + 1) * PT],
                                      start=(ck == 0), stop=(ck == 15)),
                                  "s_mm", 1)
                    w.wait("scalar", "s_mm")
                    for oi in range(gn):
                        oc = oc0 + oi
                        w.wait("scalar", "s_st", w.count["s_st"])
                        w.inc("scalar", lambda e, oi=oi, oc=oc: e.activation(
                            out=sbf_s[:],
                            in_=ps_big[:, oi * PT:(oi + 1) * PT],
                            func=AF.Identity, bias=bo_sb[:, oc:oc + 1]),
                            "s_s", 1)
                        w.wait("sync", "s_s")
                        w.inc("sync", lambda e, p=p, oc=oc: e.dma_start(
                            out=s_sat[p, oc], in_=sbf_s[:]), "s_st", 16)
                    w.wait("tensor", "s_st", w.count["s_st"])

            w.wait("sync", "s_st")
            w.wait("sync", "s_ld")
            w.wait("sync", "s_w")

            with nc.Block() as blk:
                w.run_block(blk)
            counts = dict(w.count)

        with ExitStack() as ctxb:
            bsb = lambda n, s, d: ctxb.enter_context(nc.sbuf_tensor(n, s, d))
            xg2 = bsb("b_xg", [128, HID], F32)
            out2 = bsb("b_out2", [128, HID], F32)
            sx2 = bsb("b_sx", [128, 1], F32)
            sxx2 = bsb("b_sxx", [128, 1], F32)
            lt12 = bsb("b_lt1", [128, 1], F32)
            lt22 = bsb("b_lt2", [128, 1], F32)
            lrs2 = bsb("b_lrs", [128, 1], F32)
            lit2 = bsb("b_lit", [128, 1], I32)
            xn_tm2 = bsb("b_xntm", [128, HID], BF16)
            xn_fm2 = bsb("b_xnfm", [128, 16 * PT], BF16)
            wbuf2 = [bsb(f"b_wbuf{i}", [128, 768], BF16) for i in range(2)]
            gsil = bsb("b_gsil", [128, 6 * PT], BF16)
            ubuf = bsb("b_ubuf", [128, 6 * PT], BF16)
            fuse = bsb("b_fuse", [128, 48 * PT], BF16)
            g_fm = bsb("b_gfm", [128, 16 * PT], BF16)
            s_chunk = bsb("b_schunk", [128, PT], F32)
            sg_c = bsb("b_sgc", [128, PT], F32)
            out_sb = [bsb(f"b_osb{i}", [128, HID], F32) for i in range(4)]

            ps_big = ctxb.enter_context(
                nc.psum_tensor("b_psbig", [128, 6 * PT], F32))
            ps_trb = ctxb.enter_context(
                nc.psum_tensor("b_pstrb", [128, 128], F32))
            ps_trf = ctxb.enter_context(
                nc.psum_tensor("b_pstrf", [128, 128], F32))

            w = Walk(nc, sems, counts)
            DGROUPS = [(0, 6), (6, 6), (12, 4)]

            def ln_block2(x_tile, xn_tile, sq_scratch):
                w.bare("scalar", lambda e: e.activation(
                    out=sq_scratch[:], in_=x_tile[:], func=AF.Square,
                    accum_out=sxx2[:]))
                w.bare("scalar", lambda e: e.activation(
                    out=sq_scratch[:], in_=x_tile[:], func=AF.Identity,
                    accum_out=sx2[:]))
                w.inc_last("scalar", "s_s", 1)
                w.wait("vector", "s_s")
                w.bare("vector", lambda e: e.tensor_scalar(
                    out=lt12[:], in0=sx2[:], scalar1=1.0 / 2048.0,
                    scalar2=None, op0=A.mult))
                pad2()
                w.bare("vector", lambda e: e.tensor_tensor(
                    out=lt22[:], in0=lt12[:], in1=lt12[:], op=A.mult))
                pad2()
                w.bare("vector", lambda e: e.scalar_tensor_tensor(
                    out=lt22[:], in0=sxx2[:], scalar=1.0 / 2048.0,
                    in1=lt22[:], op0=A.mult, op1=A.subtract))
                pad2()
                w.bare("vector", lambda e: e.tensor_scalar(
                    out=lt22[:], in0=lt22[:], scalar1=1e-5, scalar2=None,
                    op0=A.add))
                pad2()
                _rsqrt_newton(w, lt22, lrs2, lt12, lit2, 128, 1, pad=pad2)
                w.bare("vector", lambda e: e.tensor_scalar(
                    out=lt12[:], in0=sx2[:], scalar1=-1.0 / 2048.0,
                    scalar2=None, op0=A.mult))
                pad2()
                w.bare("vector", lambda e: e.tensor_tensor(
                    out=lt12[:], in0=lt12[:], in1=lrs2[:], op=A.mult))
                pad2()
                w.bare("vector", lambda e: e.tensor_scalar(
                    out=xn_tile[:], in0=x_tile[:], scalar1=lrs2[:, 0:1],
                    scalar2=lt12[:, 0:1], op0=A.mult, op1=A.add))

            for p in range(NPASS):
                bidx = p // 4
                r0 = (p % 4) * PT

                for g in range(4):
                    w.wait("sync", "s_v")
                    w.wait("sync", "s_s", w.count["s_s"])
                    w.wait("sync", "s_st", w.count["s_st"])
                    w.inc("sync", lambda e, b=bidx, rr=r0 + g * 128:
                          e.dma_start(out=xg2[:], in_=d_xf[b, rr:rr + 128, :]),
                          "s_ld", 16)
                    w.wait("vector", "s_ld")
                    w.wait("scalar", "s_ld", w.count["s_ld"])
                    ln_block2(xg2, xn_tm2, out2)
                    w.inc_last("vector", "s_v", 1)
                    w.wait("tensor", "s_v")
                    for ck in range(16):
                        w.inc("tensor", lambda e, ck=ck: e.matmul(
                            ps_trb[:], xn_tm2[:, ck * 128:(ck + 1) * 128],
                            iden_b[:], start=True, stop=True), "s_mm", 1)
                        w.wait("vector", "s_mm")
                        w.inc("vector", lambda e, ck=ck, g=g: e.tensor_copy(
                            out=xn_fm2[:, ck * PT + g * 128:
                                       ck * PT + g * 128 + 128],
                            in_=ps_trb[:]), "s_v", 1)
                        w.wait("tensor", "s_v")

                for cg in range(8):
                    for which, src_sc in (("g", s_gq), ("u", s_uq)):
                        for ck in range(16):
                            buf = wbuf2[ck % 2]
                            eng = "sync" if ck % 2 == 0 else "gpsimd"
                            sem = "s_w" if ck % 2 == 0 else "s_g"
                            w.wait(eng, "s_mm",
                                   w.count["s_mm"] - (6 if ck >= 2 else 0))
                            w.inc(eng, lambda e, ck=ck, cg=cg, buf=buf,
                                  src=src_sc: e.dma_start(
                                      out=buf[:],
                                      in_=src[ck * 128:(ck + 1) * 128,
                                              cg * 768:(cg + 1) * 768]),
                                  sem, 16)
                            w.wait("tensor", sem)
                            for ci in range(6):
                                w.inc("tensor", lambda e, ck=ck, ci=ci, buf=buf:
                                      e.matmul(
                                          ps_big[:, ci * PT:(ci + 1) * PT],
                                          buf[:, ci * 128:(ci + 1) * 128],
                                          xn_fm2[:, ck * PT:(ck + 1) * PT],
                                          start=(ck == 0), stop=(ck == 15)),
                                      "s_mm", 1)
                        w.wait("scalar", "s_mm")
                        w.wait("scalar", "s_v", w.count["s_v"])
                        for ci in range(6):
                            c = cg * 6 + ci
                            dst = gsil if which == "g" else ubuf
                            fn = AF.Silu if which == "g" else AF.Identity
                            sc = gs_sb if which == "g" else us_sb
                            w.inc("scalar", lambda e, ci=ci, c=c, dst=dst,
                                  fn=fn, sc=sc: e.activation(
                                      out=dst[:, ci * PT:(ci + 1) * PT],
                                      in_=ps_big[:, ci * PT:(ci + 1) * PT],
                                      func=fn, scale=sc[:, c:c + 1]), "s_s", 1)
                        w.wait("tensor", "s_s", w.count["s_s"])
                    w.wait("vector", "s_s", w.count["s_s"])
                    w.inc("vector", lambda e, cg=cg: e.tensor_tensor(
                        out=fuse[:, cg * 6 * PT:(cg + 1) * 6 * PT],
                        in0=gsil[:], in1=ubuf[:], op=A.mult), "s_v", 1)

                w.wait("tensor", "s_v", w.count["s_v"])
                for (oc0, gn) in DGROUPS:
                    for c in range(48):
                        buf = wbuf2[c % 2]
                        eng = "sync" if c % 2 == 0 else "gpsimd"
                        sem = "s_w" if c % 2 == 0 else "s_g"
                        w.wait(eng, "s_mm",
                               w.count["s_mm"] - (gn if c >= 2 else 0))
                        w.inc(eng, lambda e, c=c, oc0=oc0, gn=gn, buf=buf:
                              e.dma_start(
                                  out=buf[:, 0:gn * 128],
                                  in_=s_dq[c * 128:(c + 1) * 128,
                                           oc0 * 128:(oc0 + gn) * 128]),
                              sem, 16)
                        w.wait("tensor", sem)
                        for oi in range(gn):
                            w.inc("tensor", lambda e, c=c, oi=oi, buf=buf:
                                  e.matmul(
                                      ps_big[:, oi * PT:(oi + 1) * PT],
                                      buf[:, oi * 128:(oi + 1) * 128],
                                      fuse[:, c * PT:(c + 1) * PT],
                                      start=(c == 0), stop=(c == 47)),
                                  "s_mm", 1)
                    w.wait("scalar", "s_mm")
                    for oi in range(gn):
                        oc = oc0 + oi
                        w.inc("scalar", lambda e, oi=oi, oc=oc: e.activation(
                            out=g_fm[:, oc * PT:(oc + 1) * PT],
                            in_=ps_big[:, oi * PT:(oi + 1) * PT],
                            func=AF.Silu, scale=ds_sb[:, oc:oc + 1]), "s_s", 1)
                    w.wait("tensor", "s_s", w.count["s_s"])

                w.wait("vector", "s_s", w.count["s_s"])
                for oc in range(16):
                    w.wait("sync", "s_v", w.count["s_v"])
                    w.inc("sync", lambda e, p=p, oc=oc: e.dma_start(
                        out=s_chunk[:], in_=s_sat[p, oc]), "s_ld", 16)
                    w.wait("vector", "s_ld")
                    w.inc("vector", lambda e, oc=oc: e.tensor_tensor(
                        out=sg_c[:], in0=s_chunk[:],
                        in1=g_fm[:, oc * PT:(oc + 1) * PT], op=A.mult),
                        "s_v", 1)
                    w.wait("tensor", "s_v")
                    for q in range(4):
                        w.inc("tensor", lambda e, q=q: e.transpose(
                            ps_trf[:], sg_c[:, q * 128:(q + 1) * 128],
                            iden_f[:]), "s_mm", 1)
                        w.wait("vector", "s_mm")
                        w.inc("vector", lambda e, q=q, oc=oc: e.tensor_copy(
                            out=out_sb[q][:, oc * 128:(oc + 1) * 128],
                            in_=ps_trf[:]), "s_v", 1)
                        w.wait("tensor", "s_v")
                for q in range(4):
                    w.wait("sync", "s_v", w.count["s_v"])
                    w.wait("sync", "s_st", w.count["s_st"])
                    w.inc("sync", lambda e, b=bidx, rr=r0 + q * 128:
                          e.dma_start(out=xg2[:], in_=d_xf[b, rr:rr + 128, :]),
                          "s_ld", 16)
                    w.wait("vector", "s_ld")
                    w.wait("vector", "s_v", w.count["s_v"])
                    w.inc("vector", lambda e, q=q: e.tensor_tensor(
                        out=out2[:], in0=out_sb[q][:], in1=xg2[:], op=A.add),
                        "s_v", 1)
                    w.wait("sync", "s_v")
                    w.inc("sync", lambda e, b=bidx, rr=r0 + q * 128:
                          e.dma_start(out=d_out[b, rr:rr + 128, :],
                                      in_=out2[:]), "s_st", 16)
                    w.wait("vector", "s_st", w.count["s_st"])

            w.wait("sync", "s_st")
            w.wait("sync", "s_ld")
            w.wait("sync", "s_w")

            with nc.Block() as blk:
                w.run_block(blk)

    return nc


def _make_tables():
    import jax
    import jax.numpy as jnp
    cpu = jax.devices("cpu")[0]
    with jax.default_device(cpu):
        inv = 1.0 / 10000.0 ** (jnp.arange(0, D, 2, dtype=jnp.float32) / D)
        freqs = jnp.arange(S, dtype=jnp.float32)[:, None] * inv[None, :]
        cos = np.asarray(jnp.cos(freqs))
        sin = np.asarray(jnp.sin(freqs))
    return cos, sin


_NC_CACHE = []


def kernel(L0_attn_hidden_states, L0_attn_Wq, L0_attn_bq, L0_attn_Wk,
           L0_attn_bk, L0_attn_Wv, L0_attn_bv, L0_attn_Wo, L0_attn_bo,
           L0_attn_rand_matrix, L0_ffn_gate_w, L0_ffn_up_w, L0_ffn_down_w,
           L0_ffn_gate_scale, L0_ffn_up_scale, L0_ffn_down_scale):
    x = np.ascontiguousarray(np.asarray(L0_attn_hidden_states, np.float32))
    wq = np.asarray(L0_attn_Wq, np.float32)
    wk = np.asarray(L0_attn_Wk, np.float32)
    wv = np.asarray(L0_attn_Wv, np.float32)
    wo = np.asarray(L0_attn_Wo, np.float32)
    wqkvT = np.ascontiguousarray(np.concatenate([wq.T, wk.T, wv.T], axis=1))
    woT = np.ascontiguousarray(wo.T)
    cos, sin = _make_tables()

    maskA = np.zeros((128, 256), np.float32)
    for i in range(16):
        maskA[i * 8:(i + 1) * 8, i * 16:(i + 1) * 16] = 1.0 / 64.0
    iden = np.eye(128, dtype=np.float32)
    sel = np.zeros((128, 24 * 32), np.float32)
    for m in range(24):
        sel[:, m * 32 + (m % 32)] = 1.0
    rT = np.ascontiguousarray(np.asarray(L0_attn_rand_matrix, np.float32).T)

    def fold(vec, n):
        return np.ascontiguousarray(np.asarray(vec, np.float32).reshape(n, 128).T)

    common = {
        "wqkvT": wqkvT, "woT": woT,
        "gw": np.ascontiguousarray(np.asarray(L0_ffn_gate_w, np.float32)),
        "uw": np.ascontiguousarray(np.asarray(L0_ffn_up_w, np.float32)),
        "dw": np.ascontiguousarray(np.asarray(L0_ffn_down_w, np.float32)),
        "rT": rT, "maskA": maskA, "iden": iden, "sel": sel,
        "bq": fold(L0_attn_bq, 16), "bk": fold(L0_attn_bk, 8),
        "bv": fold(L0_attn_bv, 8), "bo": fold(L0_attn_bo, 16),
        "gs": fold(L0_ffn_gate_scale, 48), "us": fold(L0_ffn_up_scale, 48),
        "ds": fold(L0_ffn_down_scale, 16),
    }

    in_maps = []
    for c in range(NCORES):
        s0 = c * SP
        xa = np.ascontiguousarray(x[:, s0:s0 + SP, :])
        taus = np.arange(TP) + c * TP
        rows = (np.arange(H)[None, :] * 1024 + taus[:, None]).reshape(-1)
        xf = np.ascontiguousarray(x[:, rows, :])
        cseg = cos[s0:s0 + SP, :]
        sseg = sin[s0:s0 + SP, :]
        cosf = np.ascontiguousarray(np.concatenate([cseg.T, cseg.T], axis=0))
        ssinf = np.ascontiguousarray(np.concatenate([-sseg.T, sseg.T], axis=0))
        m = dict(common)
        m.update({"xa": xa, "xf": xf, "cosf": cosf, "ssinf": ssinf})
        in_maps.append(m)

    if not _NC_CACHE:
        _NC_CACHE.append(build_program())
    nc = _NC_CACHE[0]
    res = run_bass_kernel_spmd(nc, in_maps, core_ids=list(range(NCORES)))

    out = np.empty((B, S, HID), np.float32)
    for c in range(NCORES):
        o = np.asarray(res.results[c]["out"])
        taus = np.arange(TP) + c * TP
        rows = (np.arange(H)[None, :] * 1024 + taus[:, None]).reshape(-1)
        out[:, rows, :] = o
    return out


# version 10 (final)
# speedup vs baseline: 1.3960x; 1.0493x over previous
import os as _os
_v = _os.environ.get("JAX_PLATFORMS", "")
if _v and "cpu" not in _v.split(","):
    _os.environ["JAX_PLATFORMS"] = _v + ",cpu"

import numpy as np
from contextlib import ExitStack

import concourse.bass as bass
import concourse.mybir as mybir
from concourse.bass_utils import run_bass_kernel_spmd
from concourse.alu_op_type import AluOpType as A

F32 = mybir.dt.float32
I32 = mybir.dt.int32
BF16 = mybir.dt.bfloat16
AF = mybir.ActivationFunctionType

NCORES = 8
B, S, HID = 2, 16384, 2048
H, HKV, D = 16, 8, 128
FFI = 6144
SP = S // NCORES
TP = SP // 16
PT = 512
NPASS = 8

PI_LO = float(np.nextafter(np.float32(np.pi), np.float32(0.0)))
TWO_PI = 2.0 * np.pi
C1 = float(np.float32(TWO_PI))
C2 = float(np.float32(np.float64(TWO_PI) - np.float64(np.float32(TWO_PI))))
HALF_PI = float(np.float32(np.pi / 2))
HPI_ADJ = float(np.float32(np.float64(np.float32(np.pi / 2))
                           - np.float64(np.pi) / 2))


class Walk:

    ENGINES = ("sync", "vector", "scalar", "tensor", "gpsimd")

    def __init__(self, nc, sems, counts=None):
        self.nc = nc
        self.prog = {e: [] for e in self.ENGINES}
        self.sems = sems
        self.count = dict(counts) if counts else {k: 0 for k in sems}

    def emit(self, eng, fn):
        self.prog[eng].append(fn)

    def inc(self, eng, mk, sem, n):
        s = self.sems[sem]
        self.emit(eng, lambda e, mk=mk, s=s, n=n: mk(e).then_inc(s, n))
        self.count[sem] += n

    def bare(self, eng, mk):
        self.emit(eng, lambda e, mk=mk: mk(e))

    def inc_last(self, eng, sem, n):
        s = self.sems[sem]
        old = self.prog[eng].pop()
        self.emit(eng, lambda e, old=old, s=s, n=n: old(e).then_inc(s, n))
        self.count[sem] += n

    def wait(self, eng, sem, val=None):
        s = self.sems[sem]
        v = self.count[sem] if val is None else val
        if v > 0:
            self.emit(eng, lambda e, s=s, v=v: e.wait_ge(s, v))

    def run_block(self, block):
        for name in self.ENGINES:
            closures = self.prog[name]
            if not closures:
                continue

            def body(eng, closures=closures):
                for fn in closures:
                    fn(eng)

            getattr(block, name)(body)


def _rsqrt_newton(w, src, dst, scr, itile, rows, cols, pad=None):
    if pad is None:
        pad = lambda: None
    c = (slice(0, rows), slice(0, cols))
    w.bare("vector", lambda e: e.tensor_scalar(
        out=itile[c], in0=src[c].bitcast(I32), scalar1=1, scalar2=None,
        op0=A.logical_shift_right))
    pad()
    w.bare("vector", lambda e: e.tensor_scalar(
        out=itile[c], in0=itile[c], scalar1=-1, scalar2=float(0x5f3759df),
        op0=A.mult, op1=A.add))
    pad()
    w.bare("vector", lambda e: e.tensor_scalar(
        out=dst[c], in0=itile[c].bitcast(F32), scalar1=1.0, scalar2=None,
        op0=A.mult))
    pad()
    for _ in range(4):
        w.bare("vector", lambda e: e.tensor_tensor(
            out=scr[c], in0=src[c], in1=dst[c], op=A.mult))
        pad()
        w.bare("vector", lambda e: e.tensor_tensor(
            out=scr[c], in0=scr[c], in1=dst[c], op=A.mult))
        pad()
        w.bare("vector", lambda e: e.tensor_scalar(
            out=scr[c], in0=scr[c], scalar1=-0.5, scalar2=1.5,
            op0=A.mult, op1=A.add))
        pad()
        w.bare("vector", lambda e: e.tensor_tensor(
            out=dst[c], in0=dst[c], in1=scr[c], op=A.mult))
        pad()


def _quant_block(w, wf, it1, it2, it3, ft1, ft2, ft3, ft4, ft5, mki, out_bf,
                 ncols):
    sl = (slice(None), slice(0, ncols))
    bits = wf[sl].bitcast(I32)
    w.bare("vector", lambda e: e.tensor_scalar(
        out=it1[sl], in0=bits, scalar1=0x7FFFFF, scalar2=None, op0=A.bitwise_and))
    w.bare("vector", lambda e: e.tensor_scalar(
        out=it1[sl], in0=it1[sl], scalar1=float(0x7FFFF), scalar2=None, op0=A.add))
    w.bare("vector", lambda e: e.tensor_scalar(
        out=it2[sl], in0=bits, scalar1=20, scalar2=1,
        op0=A.logical_shift_right, op1=A.bitwise_and))
    w.bare("vector", lambda e: e.tensor_tensor(
        out=it1[sl], in0=it1[sl], in1=it2[sl], op=A.add))
    w.bare("vector", lambda e: e.tensor_scalar(
        out=it1[sl], in0=it1[sl], scalar1=20, scalar2=20,
        op0=A.logical_shift_right, op1=A.logical_shift_left))
    w.bare("vector", lambda e: e.tensor_scalar(
        out=it3[sl], in0=bits, scalar1=23, scalar2=23,
        op0=A.logical_shift_right, op1=A.logical_shift_left))
    w.bare("vector", lambda e: e.tensor_tensor(
        out=it3[sl], in0=it3[sl], in1=it1[sl], op=A.add))
    w.bare("vector", lambda e: e.tensor_copy(out=mki[sl], in_=ft1[sl]))
    w.bare("vector", lambda e: e.tensor_copy(out=ft2[sl], in_=mki[sl]))
    w.bare("vector", lambda e: e.tensor_tensor(
        out=ft3[sl], in0=ft1[sl], in1=ft2[sl], op=A.subtract))
    w.bare("vector", lambda e: e.tensor_scalar(
        out=ft4[sl], in0=ft3[sl], scalar1=0.5, scalar2=None, op0=A.is_gt))
    w.bare("vector", lambda e: e.tensor_tensor(
        out=ft2[sl], in0=ft2[sl], in1=ft4[sl], op=A.add))
    w.bare("vector", lambda e: e.tensor_scalar(
        out=ft4[sl], in0=ft3[sl], scalar1=0.5, scalar2=None, op0=A.is_equal))
    w.bare("vector", lambda e: e.tensor_scalar(
        out=it2[sl], in0=mki[sl], scalar1=1, scalar2=None, op0=A.bitwise_and))
    w.bare("vector", lambda e: e.tensor_copy(out=ft3[sl], in_=it2[sl]))
    w.bare("vector", lambda e: e.tensor_tensor(
        out=ft4[sl], in0=ft4[sl], in1=ft3[sl], op=A.mult))
    w.bare("vector", lambda e: e.tensor_tensor(
        out=ft2[sl], in0=ft2[sl], in1=ft4[sl], op=A.add))
    w.bare("vector", lambda e: e.tensor_scalar(
        out=ft2[sl], in0=ft2[sl], scalar1=float(2.0 ** -9), scalar2=None,
        op0=A.mult))
    w.bare("vector", lambda e: e.tensor_tensor(
        out=ft2[sl], in0=ft2[sl], in1=ft5[sl], op=A.mult))
    w.bare("vector", lambda e: e.tensor_scalar(
        out=mki[sl], in0=bits, scalar1=23, scalar2=255,
        op0=A.logical_shift_right, op1=A.bitwise_and))
    w.bare("vector", lambda e: e.tensor_scalar(
        out=mki[sl], in0=mki[sl], scalar1=120, scalar2=None, op0=A.is_le))
    w.bare("vector", lambda e: e.select(
        out=out_bf[sl], mask=mki[sl], on_true=ft2[sl],
        on_false=it3[sl].bitcast(F32)))


def build_program():
    nc = bass.Bass("TRN2", target_bir_lowering=False,
                   detect_race_conditions=False)

    d_xa = nc.dram_tensor("xa", [B, SP, HID], F32, kind="ExternalInput").ap()
    d_xf = nc.dram_tensor("xf", [B, SP, HID], F32, kind="ExternalInput").ap()
    d_wqkvT = nc.dram_tensor("wqkvT", [HID, 4096], F32, kind="ExternalInput").ap()
    d_woT = nc.dram_tensor("woT", [HID, HID], F32, kind="ExternalInput").ap()
    d_gw = nc.dram_tensor("gw", [HID, FFI], F32, kind="ExternalInput").ap()
    d_uw = nc.dram_tensor("uw", [HID, FFI], F32, kind="ExternalInput").ap()
    d_dw = nc.dram_tensor("dw", [FFI, HID], F32, kind="ExternalInput").ap()
    d_cosf = nc.dram_tensor("cosf", [128, SP], F32, kind="ExternalInput").ap()
    d_ssinf = nc.dram_tensor("ssinf", [128, SP], F32, kind="ExternalInput").ap()
    d_rT = nc.dram_tensor("rT", [128, 64], F32, kind="ExternalInput").ap()
    d_maskA = nc.dram_tensor("maskA", [128, 256], F32, kind="ExternalInput").ap()
    d_iden = nc.dram_tensor("iden", [128, 128], F32, kind="ExternalInput").ap()
    d_sel = nc.dram_tensor("sel", [128, 24 * 32], F32, kind="ExternalInput").ap()
    d_bq = nc.dram_tensor("bq", [128, 16], F32, kind="ExternalInput").ap()
    d_bk = nc.dram_tensor("bk", [128, 8], F32, kind="ExternalInput").ap()
    d_bv = nc.dram_tensor("bv", [128, 8], F32, kind="ExternalInput").ap()
    d_bo = nc.dram_tensor("bo", [128, 16], F32, kind="ExternalInput").ap()
    d_gs = nc.dram_tensor("gs", [128, 48], F32, kind="ExternalInput").ap()
    d_us = nc.dram_tensor("us", [128, 48], F32, kind="ExternalInput").ap()
    d_ds = nc.dram_tensor("ds", [128, 16], F32, kind="ExternalInput").ap()

    d_out = nc.dram_tensor("out", [B, SP, HID], F32, kind="ExternalOutput").ap()
    s_wqkv = nc.dram_tensor("s_wqkv", [HID, 4096], BF16, kind="Internal").ap()
    s_woT = nc.dram_tensor("s_woT", [HID, HID], BF16, kind="Internal").ap()
    s_gq = nc.dram_tensor("s_gq", [HID, FFI], BF16, kind="Internal").ap()
    s_uq = nc.dram_tensor("s_uq", [HID, FFI], BF16, kind="Internal").ap()
    s_dq = nc.dram_tensor("s_dq", [FFI, HID], BF16, kind="Internal").ap()
    s_sat = nc.dram_tensor("s_sat", [NPASS, 16, 128, PT], F32,
                           kind="Internal").ap()
    s_rsq = nc.dram_tensor("s_rsq", [32, PT], F32, kind="Internal").ap()

    with ExitStack() as ctx:
        sbt = lambda n, s, d: ctx.enter_context(nc.sbuf_tensor(n, s, d))
        cosf = sbt("cosf_t", [128, SP], F32)
        ssinf = sbt("ssinf_t", [128, SP], F32)
        rT_b = sbt("rT_b", [128, 64], BF16)
        maskA_b = sbt("maskA_b", [128, 256], BF16)
        iden_b = sbt("iden_b", [128, 128], BF16)
        iden_f = sbt("iden_f", [128, 128], F32)
        sel_b = sbt("sel_b", [128, 24 * 32], BF16)
        bq_sb = sbt("bq_sb", [128, 16], F32)
        bk_sb = sbt("bk_sb", [128, 8], F32)
        bv_sb = sbt("bv_sb", [128, 8], F32)
        bo_sb = sbt("bo_sb", [128, 16], F32)
        gs_sb = sbt("gs_sb", [128, 48], F32)
        us_sb = sbt("us_sb", [128, 48], F32)
        ds_sb = sbt("ds_sb", [128, 16], F32)
        cst_hpi = sbt("cst_hpi", [128, 1], F32)
        cst_adj = sbt("cst_adj", [128, 1], F32)
        fv = sbt("fence_v", [1, 8], F32)
        nc.const_aps.register(F32, HALF_PI, cst_hpi[:, 0:1])
        nc.const_aps.register(F32, HPI_ADJ, cst_adj[:, 0:1])

        sems = {k: ctx.enter_context(nc.semaphore(k)) for k in
                ("s_ld", "s_w", "s_mm", "s_v", "s_s", "s_st", "s_g", "s_c")}

        def pad2():
            for _ in range(2):
                w.bare("vector", lambda e: e.tensor_scalar(
                    out=fv[0:1, 0:4], in0=fv[0:1, 4:8], scalar1=0.0,
                    scalar2=None, op0=A.mult))

        with ExitStack() as ctxp:
            psb = lambda n, s, d: ctxp.enter_context(nc.sbuf_tensor(n, s, d))
            QW = 3072
            wfb = [psb(f"p_wf{i}", [128, QW], F32) for i in range(2)]
            wbb = [psb(f"p_wb{i}", [128, QW], BF16) for i in range(2)]
            it1 = psb("p_it1", [128, QW], I32)
            it2 = psb("p_it2", [128, QW], I32)
            it3 = psb("p_it3", [128, QW], I32)
            ft1 = psb("p_ft1", [128, QW], F32)
            ft2 = psb("p_ft2", [128, QW], F32)
            ft3 = psb("p_ft3", [128, QW], F32)
            ft4 = psb("p_ft4", [128, QW], F32)
            ft5 = psb("p_ft5", [128, QW], F32)
            mki = psb("p_mki", [128, QW], I32)
            cbb = [psb(f"p_cb{i}", [128, 4096], BF16) for i in range(2)]

            w = Walk(nc, sems)
            w.inc("vector", lambda e: e.memset(cst_hpi[:], HALF_PI), "s_c", 1)
            w.inc("vector", lambda e: e.memset(cst_adj[:], HPI_ADJ), "s_c", 1)
            w.wait("scalar", "s_c")
            w.bare("vector", lambda e: e.memset(fv[:], 0.0))
            for dst, src in ((cosf, d_cosf), (ssinf, d_ssinf),
                             (bq_sb, d_bq), (bk_sb, d_bk), (bv_sb, d_bv),
                             (bo_sb, d_bo), (gs_sb, d_gs), (us_sb, d_us),
                             (ds_sb, d_ds), (iden_f, d_iden)):
                w.inc("sync", lambda e, dst=dst, src=src: e.dma_start(
                    out=dst[:], in_=src[:]), "s_ld", 16)
            for dst, src in ((rT_b, d_rT), (maskA_b, d_maskA),
                             (iden_b, d_iden), (sel_b, d_sel)):
                w.inc("gpsimd", lambda e, dst=dst, src=src: e.dma_start(
                    out=dst[:], in_=src[:]), "s_g", 16)

            citems = ([(d_wqkvT, s_wqkv, i, 4096) for i in range(16)]
                      + [(d_woT, s_woT, i, 2048) for i in range(16)])
            cst_marks = []
            for n, (csrc, cdst, i, wd) in enumerate(citems):
                cbuf = cbb[n % 2]
                if n >= 2:
                    w.wait("gpsimd", "s_st", cst_marks[n - 2])
                w.inc("gpsimd", lambda e, i=i, src=csrc, wd=wd, cbuf=cbuf:
                      e.dma_start(out=cbuf[:, 0:wd],
                                  in_=src[i * 128:(i + 1) * 128, :]),
                      "s_g", 16)
                w.wait("sync", "s_g")
                w.inc("sync", lambda e, i=i, dst=cdst, wd=wd, cbuf=cbuf:
                      e.dma_start(out=dst[i * 128:(i + 1) * 128, :],
                                  in_=cbuf[:, 0:wd]),
                      "s_st", 16)
                cst_marks.append(w.count["s_st"])

            qchunks = []
            for (qsrc, qdst, nblk, tw) in ((d_gw, s_gq, 16, FFI),
                                           (d_uw, s_uq, 16, FFI),
                                           (d_dw, s_dq, 48, HID)):
                nch = (tw + QW - 1) // QW
                for i0 in range(nblk * nch):
                    i, cw = i0 // nch, i0 % nch
                    qchunks.append((qsrc, qdst, i, cw * QW,
                                    min(QW, tw - cw * QW)))
            NQ = len(qchunks)
            sv0, ss0 = w.count["s_v"], w.count["s_s"]
            qld_marks, qst_marks = [], []

            def qload(j):
                qsrc, qdst, i, c0, wd = qchunks[j]
                w.inc("sync", lambda e, i=i, c0=c0, src=qsrc, wd=wd,
                      wfj=wfb[j % 2]: e.dma_start(
                          out=wfj[:, 0:wd],
                          in_=src[i * 128:(i + 1) * 128, c0:c0 + wd]),
                      "s_ld", 16)
                qld_marks.append(w.count["s_ld"])

            qload(0)
            if NQ > 1:
                qload(1)
            for j in range(NQ):
                qsrc, qdst, i, c0, wd = qchunks[j]
                wfj, wbj = wfb[j % 2], wbb[j % 2]
                w.wait("scalar", "s_ld", qld_marks[j])
                if j >= 1:
                    w.wait("scalar", "s_v", sv0 + j)
                w.bare("scalar", lambda e, wd=wd, wfj=wfj: e.activation(
                    out=ft1[:, 0:wd], in_=wfj[:, 0:wd], func=AF.Abs,
                    scale=512.0))
                w.bare("scalar", lambda e, wd=wd, wfj=wfj: e.activation(
                    out=ft5[:, 0:wd], in_=wfj[:, 0:wd], func=AF.Sign))
                w.inc_last("scalar", "s_s", 1)
                w.wait("vector", "s_s", ss0 + j + 1)
                w.wait("vector", "s_ld", qld_marks[j])
                if j >= 2:
                    w.wait("vector", "s_st", qst_marks[j - 2])
                _quant_block(w, wfj, it1, it2, it3, ft1, ft2, ft3, ft4,
                             ft5, mki, wbj, wd)
                w.inc_last("vector", "s_v", 1)
                w.wait("sync", "s_v", sv0 + j + 1)
                w.inc("sync", lambda e, i=i, c0=c0, dst=qdst, wd=wd,
                      wbj=wbj: e.dma_start(
                          out=dst[i * 128:(i + 1) * 128, c0:c0 + wd],
                          in_=wbj[:, 0:wd]),
                      "s_st", 16)
                qst_marks.append(w.count["s_st"])
                if j + 2 < NQ:
                    qload(j + 2)
            w.wait("sync", "s_st")
            w.wait("sync", "s_ld")
            w.wait("gpsimd", "s_g")

            with nc.Block() as blk:
                w.run_block(blk)
            counts = dict(w.count)

        with ExitStack() as ctxa:
            asb = lambda n, s, d: ctxa.enter_context(nc.sbuf_tensor(n, s, d))
            xg = asb("a_xg", [128, HID], F32)
            xsq = asb("a_xsq", [128, HID], F32)
            sx = asb("a_sx", [128, 1], F32)
            sxx = asb("a_sxx", [128, 1], F32)
            lt1 = asb("a_lt1", [128, 1], F32)
            lt2 = asb("a_lt2", [128, 1], F32)
            lrs = asb("a_lrs", [128, 1], F32)
            lit = asb("a_lit", [128, 1], I32)
            xn_tm = asb("a_xntm", [128, HID], BF16)
            xn_fm = asb("a_xnfm", [128, 16 * PT], BF16)
            wbuf = [asb(f"a_wbuf{i}", [128, 640], BF16) for i in range(2)]
            qa = asb("a_qa", [128, PT], F32)
            qb = asb("a_qb", [128, PT], F32)
            t1 = asb("a_t1", [128, PT], F32)
            t2 = asb("a_t2", [128, PT], F32)
            sqb = asb("a_sqb", [128, PT], BF16)
            qrot = asb("a_qrot", [128, 24 * PT], BF16)
            v_il = asb("a_vil", [128, 8 * PT], BF16)
            ssqf = asb("a_ssqf", [32, PT], F32)
            rsq = asb("a_rsq", [32, PT], F32)
            nsc = asb("a_nsc", [32, PT], F32)
            nit = asb("a_nit", [32, PT], I32)
            qhat = asb("a_qhat", [128, PT], BF16)
            ppail = asb("a_pp", [128, PT], F32)
            tg1 = asb("a_tg1", [128, PT], F32)
            tg2 = asb("a_tg2", [128, PT], F32)
            tga = asb("a_tga", [64, PT], F32)
            tga2 = asb("a_tga2", [64, PT], F32)
            qp_il = asb("a_qpil", [128, 16 * PT], BF16)
            kp_il = asb("a_kpil", [128, 8 * PT], BF16)
            abuf = asb("a_abuf", [128, 256], BF16)
            bcb = asb("a_bcb", [128, PT], F32)
            vT = asb("a_vT", [128, 128], BF16)
            attn_srt = asb("a_srt", [128, 16 * PT], BF16)
            sbf_s = asb("a_sbf", [128, PT], F32)

            ps_big = ctxa.enter_context(
                nc.psum_tensor("a_psbig", [128, 5 * PT], F32))
            ps_ssq = ctxa.enter_context(
                nc.psum_tensor("a_psssq", [32, PT], F32))
            ps_aux = ctxa.enter_context(
                nc.psum_tensor("a_psaux", [128, PT], F32))
            ps_trb = ctxa.enter_context(
                nc.psum_tensor("a_pstrb", [128, 128], F32))

            w = Walk(nc, sems, counts)

            QKV_GROUPS = [(0, 5), (5, 5), (10, 5), (15, 5), (20, 5), (25, 5),
                          (30, 2)]
            WO_GROUPS = [(0, 5), (5, 5), (10, 5), (15, 1)]

            def ln_block(x_tile, xn_tile, sq_scratch):
                w.bare("scalar", lambda e: e.activation(
                    out=sq_scratch[:], in_=x_tile[:], func=AF.Square,
                    accum_out=sxx[:]))
                w.bare("scalar", lambda e: e.activation(
                    out=sq_scratch[:], in_=x_tile[:], func=AF.Identity,
                    accum_out=sx[:]))
                w.inc_last("scalar", "s_s", 1)
                w.wait("vector", "s_s")
                w.bare("vector", lambda e: e.tensor_scalar(
                    out=lt1[:], in0=sx[:], scalar1=1.0 / 2048.0, scalar2=None,
                    op0=A.mult))
                pad2()
                w.bare("vector", lambda e: e.tensor_tensor(
                    out=lt2[:], in0=lt1[:], in1=lt1[:], op=A.mult))
                pad2()
                w.bare("vector", lambda e: e.scalar_tensor_tensor(
                    out=lt2[:], in0=sxx[:], scalar=1.0 / 2048.0, in1=lt2[:],
                    op0=A.mult, op1=A.subtract))
                pad2()
                w.bare("vector", lambda e: e.tensor_scalar(
                    out=lt2[:], in0=lt2[:], scalar1=1e-5, scalar2=None,
                    op0=A.add))
                pad2()
                _rsqrt_newton(w, lt2, lrs, lt1, lit, 128, 1, pad=pad2)
                w.bare("vector", lambda e: e.tensor_scalar(
                    out=lt1[:], in0=sx[:], scalar1=-1.0 / 2048.0, scalar2=None,
                    op0=A.mult))
                pad2()
                w.bare("vector", lambda e: e.tensor_tensor(
                    out=lt1[:], in0=lt1[:], in1=lrs[:], op=A.mult))
                pad2()
                w.bare("vector", lambda e: e.tensor_scalar(
                    out=xn_tile[:], in0=x_tile[:], scalar1=lrs[:, 0:1],
                    scalar2=lt1[:, 0:1], op0=A.mult, op1=A.add))

            for p in range(NPASS):
                bidx = p // 4
                s0 = (p % 4) * PT

                for g in range(4):
                    w.wait("sync", "s_v")
                    w.wait("sync", "s_s", w.count["s_s"])
                    w.inc("sync", lambda e, b=bidx, r=s0 + g * 128:
                          e.dma_start(out=xg[:], in_=d_xa[b, r:r + 128, :]),
                          "s_ld", 16)
                    w.wait("vector", "s_ld")
                    w.wait("scalar", "s_ld", w.count["s_ld"])
                    ln_block(xg, xn_tm, xsq)
                    w.inc_last("vector", "s_v", 1)
                    w.wait("tensor", "s_v")
                    for ck in range(16):
                        w.inc("tensor", lambda e, ck=ck: e.matmul(
                            ps_trb[:], xn_tm[:, ck * 128:(ck + 1) * 128],
                            iden_b[:], start=True, stop=True), "s_mm", 1)
                        w.wait("vector", "s_mm")
                        w.inc("vector", lambda e, ck=ck, g=g: e.tensor_copy(
                            out=xn_fm[:, ck * PT + g * 128:ck * PT + g * 128 + 128],
                            in_=ps_trb[:]), "s_v", 1)
                        w.wait("tensor", "s_v")

                nqk = 0
                for (oc0, gn) in QKV_GROUPS:
                    for ck in range(16):
                        buf = wbuf[ck % 2]
                        eng = "sync" if ck % 2 == 0 else "gpsimd"
                        sem = "s_w" if ck % 2 == 0 else "s_g"
                        w.wait(eng, "s_mm",
                               w.count["s_mm"] - (gn if ck >= 2 else 0))
                        w.inc(eng, lambda e, ck=ck, oc0=oc0, gn=gn, buf=buf:
                              e.dma_start(
                                  out=buf[:, 0:gn * 128],
                                  in_=s_wqkv[ck * 128:(ck + 1) * 128,
                                             oc0 * 128:(oc0 + gn) * 128]),
                              sem, 16)
                        w.wait("tensor", sem)
                        for oi in range(gn):
                            w.inc("tensor", lambda e, ck=ck, oi=oi, buf=buf:
                                  e.matmul(
                                      ps_big[:, oi * PT:(oi + 1) * PT],
                                      buf[:, oi * 128:(oi + 1) * 128],
                                      xn_fm[:, ck * PT:(ck + 1) * PT],
                                      start=(ck == 0), stop=(ck == 15)),
                                  "s_mm", 1)
                    w.wait("scalar", "s_mm")
                    w.wait("vector", "s_mm", w.count["s_mm"])
                    for oi in range(gn):
                        oc = oc0 + oi
                        pslice = ps_big[:, oi * PT:(oi + 1) * PT]
                        if oc < 24:
                            bias_ap = (bq_sb[:, oc:oc + 1] if oc < 16
                                       else bk_sb[:, oc - 16:oc - 15])
                            w.wait("scalar", "s_v", w.count["s_v"])
                            w.bare("scalar", lambda e, ps=pslice, ba=bias_ap:
                                   e.activation(out=qa[:], in_=ps,
                                                func=AF.Identity, bias=ba))
                            w.inc_last("scalar", "s_s", 1)
                            w.wait("vector", "s_s")
                            w.bare("vector", lambda e: e.tensor_scalar(
                                out=qb[0:64, :], in0=qa[64:128, :],
                                scalar1=1.0, scalar2=None, op0=A.mult))
                            w.bare("vector", lambda e: e.tensor_scalar(
                                out=qb[64:128, :], in0=qa[0:64, :],
                                scalar1=1.0, scalar2=None, op0=A.mult))
                            cs = cosf[:, s0:s0 + PT]
                            ss = ssinf[:, s0:s0 + PT]
                            w.bare("vector", lambda e, cs=cs: e.tensor_tensor(
                                out=t1[:], in0=qa[:], in1=cs, op=A.mult))
                            w.bare("vector", lambda e, ss=ss: e.tensor_tensor(
                                out=t2[:], in0=qb[:], in1=ss, op=A.mult))
                            w.bare("vector", lambda e: e.tensor_tensor(
                                out=t1[:], in0=t1[:], in1=t2[:], op=A.add))
                            w.bare("vector", lambda e, oc=oc: e.tensor_copy(
                                out=qrot[:, oc * PT:(oc + 1) * PT], in_=t1[:]))
                            w.bare("vector", lambda e: e.tensor_tensor(
                                out=sqb[:], in0=t1[:], in1=t1[:], op=A.mult))
                            w.inc_last("vector", "s_v", 1)
                            w.wait("tensor", "s_v")
                            w.inc("tensor", lambda e, k=nqk: e.matmul(
                                ps_ssq[:], sel_b[:, k * 32:(k + 1) * 32],
                                sqb[:], start=(k == 0), stop=(k == 23)),
                                "s_mm", 1)
                            nqk += 1
                        else:
                            j = oc - 24
                            w.inc("scalar", lambda e, ps=pslice, j=j:
                                  e.activation(
                                      out=v_il[:].rearrange(
                                          "p (t j) -> p t j", j=8)[:, :, j],
                                      in_=ps, func=AF.Identity,
                                      bias=bv_sb[:, j:j + 1]), "s_s", 1)
                    w.inc("vector", lambda e: e.tensor_scalar(
                        out=fv[0:1, 0:4], in0=fv[0:1, 4:8], scalar1=0.0,
                        scalar2=None, op0=A.mult), "s_v", 1)
                    w.wait("tensor", "s_v")
                    w.wait("tensor", "s_s", w.count["s_s"])

                w.wait("vector", "s_mm", w.count["s_mm"])
                w.bare("vector", lambda e: e.tensor_scalar(
                    out=ssqf[:], in0=ps_ssq[:], scalar1=1e-12, scalar2=None,
                    op0=A.max))
                _rsqrt_newton(w, ssqf, rsq, nsc, nit, 32, PT)
                w.inc_last("vector", "s_v", 1)
                w.wait("sync", "s_v")
                w.inc("sync", lambda e: e.dma_start(out=s_rsq[:], in_=rsq[:]),
                      "s_st", 16)
                w.wait("gpsimd", "s_st", w.count["s_st"])

                for m in range(24):
                    pair, half = m // 2, m % 2
                    w.wait("gpsimd", "s_v", w.count["s_v"])
                    w.inc("gpsimd", lambda e, m=m: e.dma_start(
                        out=bcb[:],
                        in_=s_rsq[m:m + 1, :].broadcast_to((128, PT))),
                        "s_g", 16)
                    w.wait("vector", "s_g")
                    w.inc("vector", lambda e, m=m: e.tensor_tensor(
                        out=qhat[:], in0=qrot[:, m * PT:(m + 1) * PT],
                        in1=bcb[:], op=A.mult), "s_v", 1)
                    w.wait("tensor", "s_v")
                    w.inc("tensor", lambda e, half=half: e.matmul(
                        ps_big[half * 64:(half + 1) * 64,
                               half * PT:(half + 1) * PT],
                        rT_b[:], qhat[:], start=True, stop=True), "s_mm", 1)
                    if half == 1:
                        w.wait("vector", "s_mm")
                        w.bare("vector", lambda e: e.tensor_scalar(
                            out=ppail[0:64, :], in0=ps_big[0:64, 0:PT],
                            scalar1=1.0, scalar2=None, op0=A.mult))
                        w.inc("vector", lambda e: e.tensor_scalar(
                            out=ppail[64:128, :], in0=ps_big[64:128, PT:2 * PT],
                            scalar1=1.0, scalar2=None, op0=A.mult), "s_v", 1)
                        w.wait("tensor", "s_v")
                        w.bare("vector", lambda e: e.tensor_scalar(
                            out=tg1[:], in0=ppail[:], scalar1=PI_LO,
                            scalar2=None, op0=A.is_gt))
                        w.bare("vector", lambda e: e.tensor_scalar(
                            out=tg2[:], in0=ppail[:], scalar1=-PI_LO,
                            scalar2=None, op0=A.is_lt))
                        w.bare("vector", lambda e: e.tensor_tensor(
                            out=tg1[:], in0=tg1[:], in1=tg2[:], op=A.subtract))
                        w.bare("vector", lambda e: e.scalar_tensor_tensor(
                            out=tg2[:], in0=tg1[:], scalar=-C1, in1=ppail[:],
                            op0=A.mult, op1=A.add))
                        w.bare("vector", lambda e: e.scalar_tensor_tensor(
                            out=tg2[:], in0=tg1[:], scalar=-C2, in1=tg2[:],
                            op0=A.mult, op1=A.add))
                        w.bare("vector", lambda e: e.tensor_scalar(
                            out=tg2[:], in0=tg2[:], scalar1=-PI_LO,
                            scalar2=PI_LO, op0=A.max, op1=A.min))
                        w.inc_last("vector", "s_v", 1)
                        w.wait("scalar", "s_v")
                        for hh in range(2):
                            hm = pair * 2 + hh
                            if hm < 16:
                                dstt = qp_il[:].rearrange(
                                    "p (t h) -> p t h", h=16)
                                sin_dst = dstt[0:64, :, hm]
                                cos_dst = dstt[64:128, :, hm]
                            else:
                                dstt = kp_il[:].rearrange(
                                    "p (t j) -> p t j", j=8)
                                sin_dst = dstt[0:64, :, hm - 16]
                                cos_dst = dstt[64:128, :, hm - 16]
                            rrs = tg2[hh * 64:(hh + 1) * 64, :]
                            w.bare("scalar", lambda e, rrs=rrs, o=sin_dst:
                                   e.activation(out=o, in_=rrs, func=AF.Sin))
                            w.bare("scalar", lambda e, rrs=rrs: e.activation(
                                out=tga[:], in_=rrs, func=AF.Abs))
                            w.bare("scalar", lambda e: e.activation(
                                out=tga2[:], in_=tga[:], func=AF.Identity,
                                bias=HPI_ADJ))
                            w.bare("scalar", lambda e, o=cos_dst: e.activation(
                                out=o, in_=tga2[:], func=AF.Sin, scale=-1.0,
                                bias=HALF_PI))
                        w.inc_last("scalar", "s_s", 1)
                        w.wait("vector", "s_s")
                        w.wait("tensor", "s_s", w.count["s_s"])

                for g in range(32):
                    w.inc("tensor", lambda e, g=g: e.matmul(
                        ps_big[:, 0:256], kp_il[:, g * 128:(g + 1) * 128],
                        qp_il[:, g * 256:(g + 1) * 256], start=True,
                        stop=True), "s_mm", 1)
                    w.wait("vector", "s_mm")
                    w.inc("vector", lambda e: e.tensor_tensor(
                        out=abuf[:], in0=ps_big[:, 0:256], in1=maskA_b[:],
                        op=A.mult), "s_v", 1)
                    w.inc("tensor", lambda e, g=g: e.matmul(
                        ps_trb[:], v_il[:, g * 128:(g + 1) * 128],
                        iden_b[:], start=True, stop=True), "s_mm", 1)
                    w.wait("vector", "s_mm")
                    w.inc("vector", lambda e: e.tensor_copy(
                        out=vT[:], in_=ps_trb[:]), "s_v", 1)
                    w.wait("tensor", "s_v")
                    w.inc("tensor", lambda e: e.matmul(
                        ps_aux[:, 0:256], vT[:], abuf[:], start=True,
                        stop=True), "s_mm", 1)
                    w.wait("scalar", "s_mm")
                    srt4 = attn_srt[:].rearrange(
                        "p (ti g2 h) -> p ti g2 h", ti=16, g2=32)
                    w.inc("scalar", lambda e, g=g, d4=srt4: e.activation(
                        out=d4[:, :, g, :], in_=ps_aux[:, 0:256],
                        func=AF.Identity), "s_s", 1)
                    w.wait("tensor", "s_s", w.count["s_s"])

                for (oc0, gn) in WO_GROUPS:
                    for ck in range(16):
                        buf = wbuf[ck % 2]
                        w.wait("sync", "s_mm")
                        w.inc("sync", lambda e, ck=ck, oc0=oc0, gn=gn, buf=buf:
                              e.dma_start(
                                  out=buf[:, 0:gn * 128],
                                  in_=s_woT[ck * 128:(ck + 1) * 128,
                                            oc0 * 128:(oc0 + gn) * 128]),
                              "s_w", 16)
                        w.wait("tensor", "s_w")
                        for oi in range(gn):
                            w.inc("tensor", lambda e, ck=ck, oi=oi, buf=buf:
                                  e.matmul(
                                      ps_big[:, oi * PT:(oi + 1) * PT],
                                      buf[:, oi * 128:(oi + 1) * 128],
                                      attn_srt[:, ck * PT:(ck + 1) * PT],
                                      start=(ck == 0), stop=(ck == 15)),
                                  "s_mm", 1)
                    w.wait("scalar", "s_mm")
                    for oi in range(gn):
                        oc = oc0 + oi
                        w.wait("scalar", "s_st", w.count["s_st"])
                        w.inc("scalar", lambda e, oi=oi, oc=oc: e.activation(
                            out=sbf_s[:],
                            in_=ps_big[:, oi * PT:(oi + 1) * PT],
                            func=AF.Identity, bias=bo_sb[:, oc:oc + 1]),
                            "s_s", 1)
                        w.wait("sync", "s_s")
                        w.inc("sync", lambda e, p=p, oc=oc: e.dma_start(
                            out=s_sat[p, oc], in_=sbf_s[:]), "s_st", 16)
                    w.wait("tensor", "s_st", w.count["s_st"])

            w.wait("sync", "s_st")
            w.wait("sync", "s_ld")
            w.wait("sync", "s_w")

            with nc.Block() as blk:
                w.run_block(blk)
            counts = dict(w.count)

        with ExitStack() as ctxb:
            bsb = lambda n, s, d: ctxb.enter_context(nc.sbuf_tensor(n, s, d))
            xg2 = bsb("b_xg", [128, HID], F32)
            out2 = bsb("b_out2", [128, HID], F32)
            sx2 = bsb("b_sx", [128, 1], F32)
            sxx2 = bsb("b_sxx", [128, 1], F32)
            lt12 = bsb("b_lt1", [128, 1], F32)
            lt22 = bsb("b_lt2", [128, 1], F32)
            lrs2 = bsb("b_lrs", [128, 1], F32)
            lit2 = bsb("b_lit", [128, 1], I32)
            xn_tm2 = bsb("b_xntm", [128, HID], BF16)
            xn_fm2 = bsb("b_xnfm", [128, 16 * PT], BF16)
            wbuf2 = [bsb(f"b_wbuf{i}", [128, 768], BF16) for i in range(2)]
            gsil = bsb("b_gsil", [128, 6 * PT], BF16)
            ubuf = bsb("b_ubuf", [128, 6 * PT], BF16)
            fuse = bsb("b_fuse", [128, 48 * PT], BF16)
            g_fm = bsb("b_gfm", [128, 16 * PT], BF16)
            s_chunk = bsb("b_schunk", [128, PT], F32)
            sg_c = bsb("b_sgc", [128, PT], F32)
            out_sb = [bsb(f"b_osb{i}", [128, HID], F32) for i in range(4)]

            ps_big = ctxb.enter_context(
                nc.psum_tensor("b_psbig", [128, 6 * PT], F32))
            ps_trb = ctxb.enter_context(
                nc.psum_tensor("b_pstrb", [128, 128], F32))
            ps_trf = ctxb.enter_context(
                nc.psum_tensor("b_pstrf", [128, 128], F32))

            w = Walk(nc, sems, counts)
            DGROUPS = [(0, 6), (6, 6), (12, 4)]

            def ln_block2(x_tile, xn_tile, sq_scratch):
                w.bare("scalar", lambda e: e.activation(
                    out=sq_scratch[:], in_=x_tile[:], func=AF.Square,
                    accum_out=sxx2[:]))
                w.bare("scalar", lambda e: e.activation(
                    out=sq_scratch[:], in_=x_tile[:], func=AF.Identity,
                    accum_out=sx2[:]))
                w.inc_last("scalar", "s_s", 1)
                w.wait("vector", "s_s")
                w.bare("vector", lambda e: e.tensor_scalar(
                    out=lt12[:], in0=sx2[:], scalar1=1.0 / 2048.0,
                    scalar2=None, op0=A.mult))
                pad2()
                w.bare("vector", lambda e: e.tensor_tensor(
                    out=lt22[:], in0=lt12[:], in1=lt12[:], op=A.mult))
                pad2()
                w.bare("vector", lambda e: e.scalar_tensor_tensor(
                    out=lt22[:], in0=sxx2[:], scalar=1.0 / 2048.0,
                    in1=lt22[:], op0=A.mult, op1=A.subtract))
                pad2()
                w.bare("vector", lambda e: e.tensor_scalar(
                    out=lt22[:], in0=lt22[:], scalar1=1e-5, scalar2=None,
                    op0=A.add))
                pad2()
                _rsqrt_newton(w, lt22, lrs2, lt12, lit2, 128, 1, pad=pad2)
                w.bare("vector", lambda e: e.tensor_scalar(
                    out=lt12[:], in0=sx2[:], scalar1=-1.0 / 2048.0,
                    scalar2=None, op0=A.mult))
                pad2()
                w.bare("vector", lambda e: e.tensor_tensor(
                    out=lt12[:], in0=lt12[:], in1=lrs2[:], op=A.mult))
                pad2()
                w.bare("vector", lambda e: e.tensor_scalar(
                    out=xn_tile[:], in0=x_tile[:], scalar1=lrs2[:, 0:1],
                    scalar2=lt12[:, 0:1], op0=A.mult, op1=A.add))

            for p in range(NPASS):
                bidx = p // 4
                r0 = (p % 4) * PT

                for g in range(4):
                    w.wait("sync", "s_v")
                    w.wait("sync", "s_s", w.count["s_s"])
                    w.wait("sync", "s_st", w.count["s_st"])
                    w.inc("sync", lambda e, b=bidx, rr=r0 + g * 128:
                          e.dma_start(out=xg2[:], in_=d_xf[b, rr:rr + 128, :]),
                          "s_ld", 16)
                    w.wait("vector", "s_ld")
                    w.wait("scalar", "s_ld", w.count["s_ld"])
                    ln_block2(xg2, xn_tm2, out2)
                    w.inc_last("vector", "s_v", 1)
                    w.wait("tensor", "s_v")
                    for ck in range(16):
                        w.inc("tensor", lambda e, ck=ck: e.matmul(
                            ps_trb[:], xn_tm2[:, ck * 128:(ck + 1) * 128],
                            iden_b[:], start=True, stop=True), "s_mm", 1)
                        w.wait("vector", "s_mm")
                        w.inc("vector", lambda e, ck=ck, g=g: e.tensor_copy(
                            out=xn_fm2[:, ck * PT + g * 128:
                                       ck * PT + g * 128 + 128],
                            in_=ps_trb[:]), "s_v", 1)
                        w.wait("tensor", "s_v")

                for cg in range(8):
                    for which, src_sc in (("g", s_gq), ("u", s_uq)):
                        for ck in range(16):
                            buf = wbuf2[ck % 2]
                            eng = "sync" if ck % 2 == 0 else "gpsimd"
                            sem = "s_w" if ck % 2 == 0 else "s_g"
                            w.wait(eng, "s_mm",
                                   w.count["s_mm"] - (6 if ck >= 2 else 0))
                            w.inc(eng, lambda e, ck=ck, cg=cg, buf=buf,
                                  src=src_sc: e.dma_start(
                                      out=buf[:],
                                      in_=src[ck * 128:(ck + 1) * 128,
                                              cg * 768:(cg + 1) * 768]),
                                  sem, 16)
                            w.wait("tensor", sem)
                            for ci in range(6):
                                w.inc("tensor", lambda e, ck=ck, ci=ci, buf=buf:
                                      e.matmul(
                                          ps_big[:, ci * PT:(ci + 1) * PT],
                                          buf[:, ci * 128:(ci + 1) * 128],
                                          xn_fm2[:, ck * PT:(ck + 1) * PT],
                                          start=(ck == 0), stop=(ck == 15)),
                                      "s_mm", 1)
                        w.wait("scalar", "s_mm")
                        w.wait("scalar", "s_v", w.count["s_v"])
                        for ci in range(6):
                            c = cg * 6 + ci
                            dst = gsil if which == "g" else ubuf
                            fn = AF.Silu if which == "g" else AF.Identity
                            sc = gs_sb if which == "g" else us_sb
                            w.inc("scalar", lambda e, ci=ci, c=c, dst=dst,
                                  fn=fn, sc=sc: e.activation(
                                      out=dst[:, ci * PT:(ci + 1) * PT],
                                      in_=ps_big[:, ci * PT:(ci + 1) * PT],
                                      func=fn, scale=sc[:, c:c + 1]), "s_s", 1)
                        w.wait("tensor", "s_s", w.count["s_s"])
                    w.wait("vector", "s_s", w.count["s_s"])
                    w.inc("vector", lambda e, cg=cg: e.tensor_tensor(
                        out=fuse[:, cg * 6 * PT:(cg + 1) * 6 * PT],
                        in0=gsil[:], in1=ubuf[:], op=A.mult), "s_v", 1)

                w.wait("tensor", "s_v", w.count["s_v"])
                for (oc0, gn) in DGROUPS:
                    for c in range(48):
                        buf = wbuf2[c % 2]
                        eng = "sync" if c % 2 == 0 else "gpsimd"
                        sem = "s_w" if c % 2 == 0 else "s_g"
                        w.wait(eng, "s_mm",
                               w.count["s_mm"] - (gn if c >= 2 else 0))
                        w.inc(eng, lambda e, c=c, oc0=oc0, gn=gn, buf=buf:
                              e.dma_start(
                                  out=buf[:, 0:gn * 128],
                                  in_=s_dq[c * 128:(c + 1) * 128,
                                           oc0 * 128:(oc0 + gn) * 128]),
                              sem, 16)
                        w.wait("tensor", sem)
                        for oi in range(gn):
                            w.inc("tensor", lambda e, c=c, oi=oi, buf=buf:
                                  e.matmul(
                                      ps_big[:, oi * PT:(oi + 1) * PT],
                                      buf[:, oi * 128:(oi + 1) * 128],
                                      fuse[:, c * PT:(c + 1) * PT],
                                      start=(c == 0), stop=(c == 47)),
                                  "s_mm", 1)
                    w.wait("scalar", "s_mm")
                    for oi in range(gn):
                        oc = oc0 + oi
                        w.inc("scalar", lambda e, oi=oi, oc=oc: e.activation(
                            out=g_fm[:, oc * PT:(oc + 1) * PT],
                            in_=ps_big[:, oi * PT:(oi + 1) * PT],
                            func=AF.Silu, scale=ds_sb[:, oc:oc + 1]), "s_s", 1)
                    w.wait("tensor", "s_s", w.count["s_s"])

                w.wait("vector", "s_s", w.count["s_s"])
                for oc in range(16):
                    w.wait("sync", "s_v", w.count["s_v"])
                    w.inc("sync", lambda e, p=p, oc=oc: e.dma_start(
                        out=s_chunk[:], in_=s_sat[p, oc]), "s_ld", 16)
                    w.wait("vector", "s_ld")
                    w.inc("vector", lambda e, oc=oc: e.tensor_tensor(
                        out=sg_c[:], in0=s_chunk[:],
                        in1=g_fm[:, oc * PT:(oc + 1) * PT], op=A.mult),
                        "s_v", 1)
                    w.wait("tensor", "s_v")
                    for q in range(4):
                        w.inc("tensor", lambda e, q=q: e.transpose(
                            ps_trf[:], sg_c[:, q * 128:(q + 1) * 128],
                            iden_f[:]), "s_mm", 1)
                        w.wait("vector", "s_mm")
                        w.inc("vector", lambda e, q=q, oc=oc: e.tensor_copy(
                            out=out_sb[q][:, oc * 128:(oc + 1) * 128],
                            in_=ps_trf[:]), "s_v", 1)
                        w.wait("tensor", "s_v")
                for q in range(4):
                    w.wait("sync", "s_v", w.count["s_v"])
                    w.wait("sync", "s_st", w.count["s_st"])
                    w.inc("sync", lambda e, b=bidx, rr=r0 + q * 128:
                          e.dma_start(out=xg2[:], in_=d_xf[b, rr:rr + 128, :]),
                          "s_ld", 16)
                    w.wait("vector", "s_ld")
                    w.wait("vector", "s_v", w.count["s_v"])
                    w.inc("vector", lambda e, q=q: e.tensor_tensor(
                        out=out2[:], in0=out_sb[q][:], in1=xg2[:], op=A.add),
                        "s_v", 1)
                    w.wait("sync", "s_v")
                    w.inc("sync", lambda e, b=bidx, rr=r0 + q * 128:
                          e.dma_start(out=d_out[b, rr:rr + 128, :],
                                      in_=out2[:]), "s_st", 16)
                    w.wait("vector", "s_st", w.count["s_st"])

            w.wait("sync", "s_st")
            w.wait("sync", "s_ld")
            w.wait("sync", "s_w")

            with nc.Block() as blk:
                w.run_block(blk)

    return nc


def _make_tables():
    import jax
    import jax.numpy as jnp
    cpu = jax.devices("cpu")[0]
    with jax.default_device(cpu):
        inv = 1.0 / 10000.0 ** (jnp.arange(0, D, 2, dtype=jnp.float32) / D)
        freqs = jnp.arange(S, dtype=jnp.float32)[:, None] * inv[None, :]
        cos = np.asarray(jnp.cos(freqs))
        sin = np.asarray(jnp.sin(freqs))
    return cos, sin


_NC_CACHE = []


def kernel(L0_attn_hidden_states, L0_attn_Wq, L0_attn_bq, L0_attn_Wk,
           L0_attn_bk, L0_attn_Wv, L0_attn_bv, L0_attn_Wo, L0_attn_bo,
           L0_attn_rand_matrix, L0_ffn_gate_w, L0_ffn_up_w, L0_ffn_down_w,
           L0_ffn_gate_scale, L0_ffn_up_scale, L0_ffn_down_scale):
    x = np.ascontiguousarray(np.asarray(L0_attn_hidden_states, np.float32))
    wq = np.asarray(L0_attn_Wq, np.float32)
    wk = np.asarray(L0_attn_Wk, np.float32)
    wv = np.asarray(L0_attn_Wv, np.float32)
    wo = np.asarray(L0_attn_Wo, np.float32)
    wqkvT = np.ascontiguousarray(np.concatenate([wq.T, wk.T, wv.T], axis=1))
    woT = np.ascontiguousarray(wo.T)
    cos, sin = _make_tables()

    maskA = np.zeros((128, 256), np.float32)
    for i in range(16):
        maskA[i * 8:(i + 1) * 8, i * 16:(i + 1) * 16] = 1.0 / 64.0
    iden = np.eye(128, dtype=np.float32)
    sel = np.zeros((128, 24 * 32), np.float32)
    for m in range(24):
        sel[:, m * 32 + (m % 32)] = 1.0
    rT = np.ascontiguousarray(np.asarray(L0_attn_rand_matrix, np.float32).T)

    def fold(vec, n):
        return np.ascontiguousarray(np.asarray(vec, np.float32).reshape(n, 128).T)

    common = {
        "wqkvT": wqkvT, "woT": woT,
        "gw": np.ascontiguousarray(np.asarray(L0_ffn_gate_w, np.float32)),
        "uw": np.ascontiguousarray(np.asarray(L0_ffn_up_w, np.float32)),
        "dw": np.ascontiguousarray(np.asarray(L0_ffn_down_w, np.float32)),
        "rT": rT, "maskA": maskA, "iden": iden, "sel": sel,
        "bq": fold(L0_attn_bq, 16), "bk": fold(L0_attn_bk, 8),
        "bv": fold(L0_attn_bv, 8), "bo": fold(L0_attn_bo, 16),
        "gs": fold(L0_ffn_gate_scale, 48), "us": fold(L0_ffn_up_scale, 48),
        "ds": fold(L0_ffn_down_scale, 16),
    }

    in_maps = []
    for c in range(NCORES):
        s0 = c * SP
        xa = np.ascontiguousarray(x[:, s0:s0 + SP, :])
        taus = np.arange(TP) + c * TP
        rows = (np.arange(H)[None, :] * 1024 + taus[:, None]).reshape(-1)
        xf = np.ascontiguousarray(x[:, rows, :])
        cseg = cos[s0:s0 + SP, :]
        sseg = sin[s0:s0 + SP, :]
        cosf = np.ascontiguousarray(np.concatenate([cseg.T, cseg.T], axis=0))
        ssinf = np.ascontiguousarray(np.concatenate([-sseg.T, sseg.T], axis=0))
        m = dict(common)
        m.update({"xa": xa, "xf": xf, "cosf": cosf, "ssinf": ssinf})
        in_maps.append(m)

    if not _NC_CACHE:
        _NC_CACHE.append(build_program())
    nc = _NC_CACHE[0]
    res = run_bass_kernel_spmd(nc, in_maps, core_ids=list(range(NCORES)))

    out = np.empty((B, S, HID), np.float32)
    for c in range(NCORES):
        o = np.asarray(res.results[c]["out"])
        taus = np.arange(TP) + c * TP
        rows = (np.arange(H)[None, :] * 1024 + taus[:, None]).reshape(-1)
        out[:, rows, :] = o
    return out
